# version 04
# baseline (speedup 1.0000x reference)
from contextlib import ExitStack
import math
import numpy as np
import ml_dtypes

import concourse.bass as bass
import concourse.mybir as mybir
from concourse.alu_op_type import AluOpType
from concourse.bass_utils import run_bass_kernel_spmd

F32 = mybir.dt.float32
BF16 = mybir.dt.bfloat16
AF = mybir.ActivationFunctionType
AX = mybir.AxisListType

D = 2048
NT = 8
T = NT * 128
NEG = -1.0e30
EPS_LN = 1e-5
EPS_RMS = 1e-6
ATTN_SCALE = 0.08838834764831845

ENGS = ("sp", "act", "pe", "dve", "pl")
_SECT = {"sp": "sync", "act": "scalar", "pe": "tensor", "dve": "vector", "pl": "gpsimd"}


class Buf:
    __slots__ = ("w", "r", "name")

    def __init__(self, name=""):
        self.w = None
        self.r = []
        self.name = name


class TK:

    def __init__(self, nc, ctx):
        self.nc = nc
        self.ctx = ctx
        self.q = {"sp": nc.sync, "act": nc.scalar, "pe": nc.tensor,
                  "dve": nc.vector, "pl": nc.gpsimd}
        self.sem = {s: ctx.enter_context(nc.semaphore(f"S{s}")) for s in ENGS}
        self.cnt = {s: 0 for s in ENGS}
        self.hw = {e: {} for e in ENGS}
        self.ops = {e: [] for e in ENGS}

    def _sem(self, name):
        if name not in self.sem:
            self.sem[name] = self.ctx.enter_context(
                self.nc.semaphore(f"S{name}"))
            self.cnt[name] = 0
        return self.sem[name]

    def emit(self, eng, fn, reads=(), writes=(), dma=None):
        deps = {}

        def dep(tok):
            if tok is not None:
                s2, c2 = tok
                if c2 > deps.get(s2, 0):
                    deps[s2] = c2

        for b in reads:
            dep(b.w)
        for b in writes:
            dep(b.w)
            for tok in b.r:
                dep(tok)
        waits = []
        hw = self.hw[eng]
        for s2, c2 in deps.items():
            if hw.get(s2, 0) < c2:
                waits.append((s2, c2))
                hw[s2] = c2
        if dma is None:
            sname, inc = eng, 1
        else:
            sname, inc = "D_" + dma, 16
            self._sem(sname)
        self.cnt[sname] += inc
        tok = (sname, self.cnt[sname])
        self.ops[eng].append((waits, fn, inc, sname))
        for b in reads:
            b.r.append(tok)
        for b in writes:
            b.w = tok
            b.r = []
        return tok

    def wait_all_on(self, eng):
        waits = []
        hw = self.hw[eng]
        for s2, c2 in self.cnt.items():
            if s2 != eng and c2 > hw.get(s2, 0):
                waits.append((s2, c2))
                hw[s2] = c2
        self.cnt[eng] += 1
        self.ops[eng].append((waits, lambda q: q.nop(), 1, eng))

    def finish(self):
        waits = []
        hw = self.hw["sp"]
        for s2, c2 in self.cnt.items():
            if s2 != "sp" and c2 > hw.get(s2, 0):
                waits.append((s2, c2))
                hw[s2] = c2
        self.cnt["sp"] += 1
        self.ops["sp"].append((waits, lambda q: q.nop(), 1, "sp"))

        with self.nc.Block() as block:
            for e in ENGS:
                sect = getattr(block, _SECT[e])
                ops = self.ops[e]
                sem = self.sem

                def section(q, ops=ops):
                    for (waits, fn, inc, sname) in ops:
                        for (s2, c2) in waits:
                            q.wait_ge(sem[s2], c2)
                        inst = fn(q)
                        if inst is not None:
                            inst.then_inc(sem[sname], inc)

                sect(section)




class Mem:
    _seq = 0

    def __init__(self, nc, ctx):
        self.nc = nc
        self.ctx = ctx
        self.bpp = 0
        self.psum_banks = 0
        Mem._seq += 1
        self.tag = f"_{Mem._seq}"

    def sb(self, name, shape, dtype):
        t = self.ctx.enter_context(
            self.nc.sbuf_tensor(name + self.tag, list(shape), dtype))
        esz = 2 if dtype == BF16 else 4
        n = 1
        for s in shape[1:]:
            n *= s
        self.bpp += n * esz
        return t[:], Buf(name)

    def ps(self, name, shape, dtype):
        t = self.ctx.enter_context(
            self.nc.psum_tensor(name + self.tag, list(shape), dtype))
        esz = 2 if dtype == BF16 else 4
        n = 1
        for s in shape[1:]:
            n *= s
        self.psum_banks += (n * esz + 2047) // 2048
        return t[:], Buf(name)


def emit_identity(tk, mem):
    ident, b = mem.sb("identB", [128, 128], BF16)
    tk.emit("pl", lambda q: q.memset(ident, 0.0), writes=[b])
    tk.emit("pl", lambda q: q.affine_select(
        out=ident, in_=ident, compare_op=AluOpType.not_equal, fill=1.0,
        base=0, pattern=[[-1, 128]], channel_multiplier=1),
        reads=[b], writes=[b])
    return ident, b


def emit_ln(tk, st, X, bX, xln, bxln, j1, j2):
    tk.emit("dve", lambda q: q.tensor_scalar(
        j1[0], X, 1.0 / D, None, op0=AluOpType.mult, op1=AluOpType.add,
        accum_out=st["mean"][0]), reads=[bX], writes=[st["mean"][1], j1[1]])
    tk.emit("act", lambda q: q.activation(
        j2[0], X, AF.Square, accum_out=st["ssq"][0]),
        reads=[bX], writes=[st["ssq"][1], j2[1]])
    tk.emit("dve", lambda q: q.tensor_tensor(
        st["var"][0], st["mean"][0], st["mean"][0], op=AluOpType.mult),
        reads=[st["mean"][1]], writes=[st["var"][1]])
    tk.emit("dve", lambda q: q.scalar_tensor_tensor(
        st["var"][0], st["ssq"][0], 1.0 / D, st["var"][0],
        op0=AluOpType.mult, op1=AluOpType.subtract),
        reads=[st["ssq"][1], st["var"][1]], writes=[st["var"][1]])
    tk.emit("act", lambda q: q.activation(
        st["rstd"][0], st["var"][0], AF.Ln, bias=st["epln"][0]),
        reads=[st["var"][1], st["epln"][1]], writes=[st["rstd"][1]])
    tk.emit("act", lambda q: q.activation(
        st["rstd"][0], st["rstd"][0], AF.Exp, scale=-0.5),
        reads=[st["rstd"][1]], writes=[st["rstd"][1]])
    tk.emit("dve", lambda q: q.scalar_tensor_tensor(
        st["negmb"][0], st["mean"][0], -1.0, st["rstd"][0],
        op0=AluOpType.mult, op1=AluOpType.mult),
        reads=[st["mean"][1], st["rstd"][1]], writes=[st["negmb"][1]])
    tk.emit("act", lambda q: q.activation(
        xln, X, AF.Identity, bias=st["negmb"][0], scale=st["rstd"][0]),
        reads=[bX, st["rstd"][1], st["negmb"][1]], writes=[bxln])


def mk_stats(mem):
    st = {n: mem.sb("st_" + n, [128, 1], F32) for n in
          ("mean", "ssq", "var", "rstd", "negmb")}
    return st


def const_col(tk, mem, name, value):
    ap, b = mem.sb(name, [128, 1], F32)
    tk.emit("pl", lambda q: q.memset(ap, value), writes=[b])
    return ap, b


def emit_tr16(tk, ident, b_id, src, bsrc, dst, bdst, nblk, psT, width=128):
    done = 0
    bank = 0
    while done < nblk:
        k = min(8, nblk - done)
        pb, bpb = psT[bank % len(psT)]

        def tg(q, src=src, pb=pb, done=done, k=k):
            last = None
            for i in range(k):
                last = q.matmul(pb[:, i * 128:(i + 1) * 128],
                                src[:, (done + i) * 128:(done + i + 1) * 128],
                                ident, start=True, stop=True,
                                is_transpose=True, skip_group_check=True)
            return last
        tk.emit("pe", tg, reads=[bsrc, b_id], writes=[bpb])
        tk.emit("dve", lambda q, pb=pb, done=done, k=k: q.tensor_copy(
            dst[:, done * 128:(done + k) * 128], pb[:, 0:k * 128]),
            reads=[bpb], writes=[bdst])
        done += k
        bank += 1



def emit_A(nc, tk, ctx, layer, io, pfx=""):
    nk = 8 if layer == 0 else 16
    xs = io["xs"]; Wq = io["Wq"]; Wk = io["Wk"]; Wv = io["Wv"]
    qT_o = io["qT"]; kT_o = io["kT"]; v_o = io["v"]
    if layer == 0:
        cosT = io["cosT"]; sinT = io["sinT"]
        qns = io["qns"]; kns = io["kns"]
    mem = Mem(nc, ctx)
    ident, b_id = emit_identity(tk, mem)
    st = mk_stats(mem)
    st["epln"] = const_col(tk, mem, "epln", EPS_LN)
    eprms = const_col(tk, mem, "eprms", EPS_RMS)
    lsc = const_col(tk, mem, "lsc", math.log(ATTN_SCALE))

    X = [mem.sb(f"X{i}", [128, D], F32) for i in range(2)]
    XLN, bXLN = mem.sb("XLN", [128, D], BF16)
    XLT = [mem.sb(f"XLT{i}", [128, D], BF16) for i in range(2)]
    psT = [mem.ps(f"psT{i}", [128, 1024], BF16) for i in range(2)]
    psM = [mem.ps(f"psM{i}", [128, 512], F32) for i in range(4)]

    if layer == 0:
        WqS, bWq = mem.sb("WqS", [128, 16 * 2048], BF16)
        WkS, bWk = mem.sb("WkS", [128, 16 * nk * 128], BF16)
        WvS, bWv = mem.sb("WvS", [128, 16 * nk * 128], BF16)
        tk.emit("pl", lambda q: q.dma_start(
            out=WqS.rearrange("p (c n) -> p c n", c=16),
            in_=Wq.rearrange("(c p) n -> p c n", p=128)), writes=[bWq], dma=pfx + "wq")
        tk.emit("pl", lambda q: q.dma_start(
            out=WkS.rearrange("p (c n) -> p c n", c=16),
            in_=Wk.rearrange("(c p) n -> p c n", p=128)), writes=[bWk], dma=pfx + "wk")
        tk.emit("pl", lambda q: q.dma_start(
            out=WvS.rearrange("p (c n) -> p c n", c=16),
            in_=Wv.rearrange("(c p) n -> p c n", p=128)), writes=[bWv], dma=pfx + "wv")

        COS, bCOS = mem.sb("COS", [128, NT * 64], F32)
        SIN, bSIN = mem.sb("SIN", [128, NT * 64], F32)
        tk.emit("sp", lambda q: q.dma_start(
            out=COS.rearrange("p (r j) -> p r j", r=NT),
            in_=cosT.rearrange("(r p) j -> p r j", p=128)), writes=[bCOS], dma=pfx + "cos")
        tk.emit("sp", lambda q: q.dma_start(
            out=SIN.rearrange("p (r j) -> p r j", r=NT),
            in_=sinT.rearrange("(r p) j -> p r j", p=128)), writes=[bSIN], dma=pfx + "sin")
        NSQ, bNSQ = mem.sb("NSQ", [1, 128], BF16)
        NSK, bNSK = mem.sb("NSK", [1, 128], BF16)
        tk.emit("pl", lambda q: q.dma_start(out=NSQ, in_=qns), writes=[bNSQ], dma=pfx + "nsq")
        tk.emit("pl", lambda q: q.dma_start(out=NSK, in_=kns), writes=[bNSK], dma=pfx + "nsk")
        ONE1, bONE = mem.sb("ONE1", [1, 128], BF16)
        tk.emit("pl", lambda q: q.memset(ONE1, 1.0), writes=[bONE])
        QSB, bQSB = mem.sb("QSB", [128, 128], F32)
        KSB, bKSB = mem.sb("KSB", [128, 128], F32)
        for (SB, bSB, NS, bNS) in ((QSB, bQSB, NSQ, bNSQ), (KSB, bKSB, NSK, bNSK)):
            pb, bpb = psM[0]
            tk.emit("pe", lambda q, NS=NS, pb=pb: q.matmul(
                pb[:, 0:128], ONE1, NS, start=True, stop=True),
                reads=[bONE, bNS], writes=[bpb])
            tk.emit("dve", lambda q, SB=SB, pb=pb: q.tensor_copy(SB, pb[:, 0:128]),
                    reads=[bpb], writes=[bSB])

        Q, bQ = mem.sb("Q", [128, 2048], F32)
        K, bK = mem.sb("K", [128, nk * 128], F32)
        QSQ, bQSQ = mem.sb("QSQ", [128, 2048], F32)
        SSH, bSSH = mem.sb("SSH", [128, 16], F32)
        RSD, bRSD = mem.sb("RSD", [128, 16], F32)
        SSHK, bSSHK = mem.sb("SSHK", [128, 8], F32)
        RSDK, bRSDK = mem.sb("RSDK", [128, 8], F32)
        RT1, bRT1 = mem.sb("RT1", [128, 1024], F32)
        RT2, bRT2 = mem.sb("RT2", [128, 1024], F32)
        QR, bQR = mem.sb("QR", [128, 2048], BF16)
        KR, bKR = mem.sb("KR", [128, nk * 128], BF16)
        VT, bVT = mem.sb("VT", [128, nk * 128], BF16)
        STG, bSTG = XLN, bXLN
        KSTG, bKSTG = mem.sb("KSTG", [128, nk * 128], BF16)

        for r in range(NT):
            Xa, bXa = X[r % 2]
            tk.emit("sp", lambda q, r=r, Xa=Xa: q.dma_start(
                out=Xa, in_=xs[r * 128:(r + 1) * 128, :]),
                writes=[bXa], dma=pfx + "x%d" % (r % 2))
            emit_ln(tk, st, Xa, bXa, XLN, bXLN, (QR, bQR), (QSQ, bQSQ))
            XLTa, bXLT = XLT[r % 2]
            emit_tr16(tk, ident, b_id, XLN, bXLN, XLTa, bXLT, 16, psT)

            def proj(W, bW, wwid, slab, pb, bpb):
                nsl = wwid // 512

                def g(q, W=W, slab=slab, pb=pb, nsl=nsl, XLTa=XLTa):
                    last = None
                    for c in range(16):
                        last = q.matmul(
                            pb, XLTa[:, c * 128:(c + 1) * 128],
                            W[:, (c * nsl + slab) * 512:(c * nsl + slab + 1) * 512],
                            start=(c == 0), stop=(c == 15))
                    return last
                tk.emit("pe", g, reads=[bXLT, bW], writes=[bpb])

            for s in range(4):
                pb, bpb = psM[s % 2]
                proj(WqS, bWq, 2048, s, pb, bpb)
                tk.emit("dve", lambda q, pb=pb, s=s: q.tensor_copy(
                    Q[:, s * 512:(s + 1) * 512], pb), reads=[bpb], writes=[bQ])
            for s in range(2):
                pb, bpb = psM[2 + s]
                proj(WkS, bWk, 1024, s, pb, bpb)
                tk.emit("dve", lambda q, pb=pb, s=s: q.tensor_copy(
                    K[:, s * 512:(s + 1) * 512], pb), reads=[bpb], writes=[bK])
            for s in range(2):
                pb, bpb = psM[s]
                proj(WvS, bWv, 1024, s, pb, bpb)
                tk.emit("dve", lambda q, pb=pb, s=s: q.tensor_copy(
                    VT[:, s * 512:(s + 1) * 512], pb), reads=[bpb], writes=[bVT])

            tk.emit("act", lambda q: q.activation(QSQ, Q, AF.Square),
                    reads=[bQ], writes=[bQSQ])
            tk.emit("dve", lambda q: q.tensor_reduce(
                SSH, QSQ.rearrange("p (n h) -> p n h", n=16), axis=AX.X,
                op=AluOpType.add), reads=[bQSQ], writes=[bSSH])
            tk.emit("act", lambda q: q.activation(
                RSD, SSH, AF.Ln, bias=eprms[0], scale=1.0 / 128),
                reads=[bSSH, eprms[1]], writes=[bRSD])
            tk.emit("act", lambda q: q.activation(
                RSD, RSD, AF.Exp, scale=-0.5, bias=lsc[0]),
                reads=[bRSD, lsc[1]], writes=[bRSD])
            tk.emit("act", lambda q: q.activation(
                QSQ[:, 0:1024], K, AF.Square), reads=[bK], writes=[bQSQ])
            tk.emit("dve", lambda q: q.tensor_reduce(
                SSHK, QSQ[:, 0:1024].rearrange("p (n h) -> p n h", n=8),
                axis=AX.X, op=AluOpType.add), reads=[bQSQ], writes=[bSSHK])
            tk.emit("act", lambda q: q.activation(
                RSDK, SSHK, AF.Ln, bias=eprms[0], scale=1.0 / 128),
                reads=[bSSHK, eprms[1]], writes=[bRSDK])
            tk.emit("act", lambda q: q.activation(
                RSDK, RSDK, AF.Exp, scale=-0.5), reads=[bRSDK], writes=[bRSDK])
            tk.emit("dve", lambda q: q.tensor_tensor(
                QSQ.rearrange("p (n h) -> p n h", n=16),
                Q.rearrange("p (n h) -> p n h", n=16),
                QSB.unsqueeze(1).broadcast_to([128, 16, 128]),
                op=AluOpType.mult), reads=[bQ, bQSB], writes=[bQSQ])

            def rope(S, bS, nh, RS, bRS, dst, bdst):
                s4 = S[:, 0:nh * 128].rearrange("p (n t j) -> p n t j", n=nh, t=2)
                q1v = s4[:, :, 0:1, :].squeeze(2)
                q2v = s4[:, :, 1:2, :].squeeze(2)
                cb = COS[:, r * 64:(r + 1) * 64].unsqueeze(1).broadcast_to(
                    [128, nh, 64])
                sb2 = SIN[:, r * 64:(r + 1) * 64].unsqueeze(1).broadcast_to(
                    [128, nh, 64])
                rv = RT1[:, 0:nh * 64].rearrange("p (n j) -> p n j", n=nh)
                rv2 = RT2[:, 0:nh * 64].rearrange("p (n j) -> p n j", n=nh)
                d4 = dst[:, 0:nh * 128].rearrange("p (n t j) -> p n t j", n=nh, t=2)
                d1v = d4[:, :, 0:1, :].squeeze(2)
                d2v = d4[:, :, 1:2, :].squeeze(2)
                rsb = RS[:, 0:nh].unsqueeze(2).broadcast_to([128, nh, 64])
                tk.emit("dve", lambda q: q.tensor_tensor(rv, q1v, cb,
                        op=AluOpType.mult), reads=[bS, bCOS], writes=[bRT1])
                tk.emit("dve", lambda q: q.tensor_tensor(rv2, q2v, sb2,
                        op=AluOpType.mult), reads=[bS, bSIN], writes=[bRT2])
                tk.emit("dve", lambda q: q.tensor_tensor(rv, rv, rv2,
                        op=AluOpType.subtract), reads=[bRT1, bRT2], writes=[bRT1])
                tk.emit("dve", lambda q: q.tensor_tensor(d1v, rv, rsb,
                        op=AluOpType.mult), reads=[bRT1, bRS], writes=[bdst])
                tk.emit("dve", lambda q: q.tensor_tensor(rv, q2v, cb,
                        op=AluOpType.mult), reads=[bS, bCOS], writes=[bRT1])
                tk.emit("dve", lambda q: q.tensor_tensor(rv2, q1v, sb2,
                        op=AluOpType.mult), reads=[bS, bSIN], writes=[bRT2])
                tk.emit("dve", lambda q: q.tensor_tensor(rv, rv, rv2,
                        op=AluOpType.add), reads=[bRT1, bRT2], writes=[bRT1])
                tk.emit("dve", lambda q: q.tensor_tensor(d2v, rv, rsb,
                        op=AluOpType.mult), reads=[bRT1, bRS], writes=[bdst])

            rope(QSQ, bQSQ, 16, RSD, bRSD, QR, bQR)
            tk.emit("dve", lambda q: q.tensor_tensor(
                QSQ[:, 0:1024].rearrange("p (n h) -> p n h", n=8),
                K.rearrange("p (n h) -> p n h", n=8),
                KSB.unsqueeze(1).broadcast_to([128, 8, 128]),
                op=AluOpType.mult), reads=[bK, bKSB], writes=[bQSQ])
            rope(QSQ, bQSQ, 8, RSDK, bRSDK, KR, bKR)

            emit_tr16(tk, ident, b_id, QR, bQR, STG, bSTG, 16, psT)
            tk.emit("sp", lambda q, r=r: q.dma_start(out=qT_o[r], in_=STG),
                    reads=[bSTG], dma=pfx + "qst")
            emit_tr16(tk, ident, b_id, KR, bKR, KSTG, bKSTG, 8, psT)
            tk.emit("sp", lambda q, r=r: q.dma_start(out=kT_o[r], in_=KSTG),
                    reads=[bKSTG], dma=pfx + "kst")
            tk.emit("sp", lambda q, r=r: q.dma_start(out=v_o[r], in_=VT),
                    reads=[bVT], dma=pfx + "vst")
    else:
        XLTF, bXLTF = mem.sb("XLTF", [128, NT * 2048], BF16)
        J1, bJ1 = mem.sb("J1", [128, D], BF16)
        J2, bJ2 = mem.sb("J2", [128, D], F32)
        for r in range(NT):
            Xa, bXa = X[r % 2]
            tk.emit("sp", lambda q, r=r, Xa=Xa: q.dma_start(
                out=Xa, in_=xs[r * 128:(r + 1) * 128, :]),
                writes=[bXa], dma=pfx + "x%d" % (r % 2))
            emit_ln(tk, st, Xa, bXa, XLN, bXLN, (J1, bJ1), (J2, bJ2))
            emit_tr16(tk, ident, b_id, XLN, bXLN,
                      XLTF[:, r * 2048:(r + 1) * 2048], bXLTF, 16, psT)

        WH = [mem.sb(f"WH{i}", [128, 16 * 128], BF16) for i in range(2)]
        HSTG, bHSTG = mem.sb("HSTG", [128, 1024], BF16)
        VSTG = [mem.sb(f"VSTG{i}", [128, 2048], BF16) for i in range(NT)]
        xv = XLTF.rearrange("p (r c t) -> p c r t", c=16, t=128)

        def headmm(Wsrc, n, bWH, WHa, wkey, out_dst, bdst, scale, to_v=False):
            tk.emit("pl", lambda q, Wsrc=Wsrc, n=n, WHa=WHa: q.dma_start(
                out=WHa.rearrange("p (c h) -> p c h", c=16),
                in_=Wsrc[n].rearrange("(c p) h -> p c h", p=128)),
                writes=[bWH], dma=wkey)
            for m in range(2):
                pb, bpb = psM[m]

                def g(q, WHa=WHa, m=m, pb=pb):
                    last = None
                    for c in range(16):
                        last = q.matmul(
                            pb, WHa[:, c * 128:(c + 1) * 128],
                            xv[:, c, m * 4:(m + 1) * 4, :],
                            start=(c == 0), stop=(c == 15))
                    return last
                tk.emit("pe", g, reads=[bXLTF, bWH], writes=[bpb])
                tk.emit("act", lambda q, pb=pb, m=m, scale=scale: q.activation(
                    HSTG[:, m * 512:(m + 1) * 512], pb, AF.Copy, scale=scale),
                    reads=[bpb], writes=[bHSTG])
            if to_v:
                for r in range(NT):
                    pb, bpb = psT[r % 2]
                    tk.emit("pe", lambda q, r=r, pb=pb: q.matmul(
                        pb[:, 0:128], HSTG[:, r * 128:(r + 1) * 128], ident,
                        start=True, stop=True, is_transpose=True,
                        skip_group_check=True),
                        reads=[bHSTG, b_id], writes=[bpb])
                    Va, bVa = VSTG[r]
                    tk.emit("dve", lambda q, pb=pb, Va=Va, n=n: q.tensor_copy(
                        Va[:, n * 128:(n + 1) * 128], pb[:, 0:128]),
                        reads=[bpb], writes=[bVa])
            else:
                tk.emit("sp", lambda q, out_dst=out_dst, n=n: q.dma_start(
                    out=out_dst[n], in_=HSTG), reads=[bHSTG], dma=pfx + "hst")

        for n in range(16):
            WHa, bWH = WH[n % 2]
            headmm(Wq, n, bWH, WHa, "wh%d" % (n % 2), qT_o, None, ATTN_SCALE)
            WHa2, bWH2 = WH[(n + 1) % 2]
            headmm(Wk, n, bWH2, WHa2, "wh%d" % ((n + 1) % 2), kT_o, None, 1.0)
            headmm(Wv, n, bWH, WHa, "wh%d" % (n % 2), None, None, 1.0, to_v=True)
        for r in range(NT):
            Va, bVa = VSTG[r]
            tk.emit("sp", lambda q, r=r, Va=Va: q.dma_start(
                out=v_o[r], in_=Va), reads=[bVa], dma=pfx + "vs%d" % r)

    assert mem.bpp <= 207000, f"SBUF over budget: {mem.bpp}"


def build_A(layer):
    nc = bass.Bass("TRN2", target_bir_lowering=False)
    nk = 8 if layer == 0 else 16
    io = {}
    io["xs"] = nc.dram_tensor("xs", [T, D], F32, kind="ExternalInput").ap()
    if layer == 0:
        io["Wq"] = nc.dram_tensor("Wq", [D, 2048], F32, kind="ExternalInput").ap()
        io["Wk"] = nc.dram_tensor("Wk", [D, nk * 128], F32, kind="ExternalInput").ap()
        io["Wv"] = nc.dram_tensor("Wv", [D, nk * 128], F32, kind="ExternalInput").ap()
        io["cosT"] = nc.dram_tensor("cosT", [T, 64], F32, kind="ExternalInput").ap()
        io["sinT"] = nc.dram_tensor("sinT", [T, 64], F32, kind="ExternalInput").ap()
        io["qns"] = nc.dram_tensor("qns", [1, 128], F32, kind="ExternalInput").ap()
        io["kns"] = nc.dram_tensor("kns", [1, 128], F32, kind="ExternalInput").ap()
        io["qT"] = nc.dram_tensor("qT", [NT, 128, 2048], BF16, kind="ExternalOutput").ap()
        io["kT"] = nc.dram_tensor("kT", [NT, 128, nk * 128], BF16, kind="ExternalOutput").ap()
    else:
        io["Wq"] = nc.dram_tensor("Wq", [16, D, 128], F32, kind="ExternalInput").ap()
        io["Wk"] = nc.dram_tensor("Wk", [nk, D, 128], F32, kind="ExternalInput").ap()
        io["Wv"] = nc.dram_tensor("Wv", [nk, D, 128], F32, kind="ExternalInput").ap()
        io["qT"] = nc.dram_tensor("qT", [16, 128, T], BF16, kind="ExternalOutput").ap()
        io["kT"] = nc.dram_tensor("kT", [nk, 128, T], BF16, kind="ExternalOutput").ap()
    io["v"] = nc.dram_tensor("v", [NT, 128, nk * 128], BF16, kind="ExternalOutput").ap()
    with ExitStack() as gctx:
        tk = TK(nc, gctx)
        with ExitStack() as ctx:
            emit_A(nc, tk, ctx, layer, io)
        tk.finish()
    return nc




def host_prep_A(layer, inputs, xs_by_core):
    nc = build_A(layer)
    in_maps = []
    if layer == 0:
        qk = np.ascontiguousarray(
            inputs["L0_attn_q_kernel"].transpose(1, 0, 2).reshape(D, 2048))
        kk = np.ascontiguousarray(
            inputs["L0_attn_kv_kernel"][0].transpose(1, 0, 2).reshape(D, 1024))
        vk = np.ascontiguousarray(
            inputs["L0_attn_kv_kernel"][1].transpose(1, 0, 2).reshape(D, 1024))
        qns = inputs["L0_attn_q_norm_scale"].reshape(1, 128)
        kns = inputs["L0_attn_k_norm_scale"].reshape(1, 128)
        for c in range(8):
            h = c % 2
            pos = (np.arange(T) // 128 * 2 + h) * 128 + (np.arange(T) % 128)
            j = np.arange(64)
            ts = 10000.0 ** (2 * j / 128.0)
            ang = pos[:, None] / ts[None, :]
            in_maps.append({
                "xs": xs_by_core[c], "Wq": qk, "Wk": kk, "Wv": vk,
                "cosT": np.cos(ang).astype(np.float32),
                "sinT": np.sin(ang).astype(np.float32),
                "qns": qns, "kns": kns})
    else:
        wq = np.ascontiguousarray(
            inputs["L1_attn_W_Q"].transpose(0, 2, 1))
        wk = np.ascontiguousarray(inputs["L1_attn_W_K"].transpose(0, 2, 1))
        wv = np.ascontiguousarray(inputs["L1_attn_W_V"].transpose(0, 2, 1))
        for c in range(8):
            in_maps.append({"xs": xs_by_core[c], "Wq": wq, "Wk": wk, "Wv": wv})
    return nc, in_maps


def shard_x(x):
    xb = x.reshape(4, 16, 128, D)
    out = []
    for c in range(8):
        b, h = c // 2, c % 2
        out.append(np.ascontiguousarray(
            xb[b, h::2].reshape(T, D)))
    return out


def unshard_x(parts):
    x = np.zeros((4, 16, 128, D), dtype=parts[0].dtype)
    for c in range(8):
        b, h = c // 2, c % 2
        x[b, h::2] = parts[c].reshape(NT, 128, D)
    return x.reshape(4, 2048, D)





def wdt_of(layer, r):
    return (min(r, 4) + 1) if layer == 0 else (r + 1)


def emit_B(nc, tk, ctx, layer, io, pfx=""):
    nk = 8 if layer == 0 else 16
    qT_i = io["qT"]; kTo_i = io["kTo"]; kTp_i = io["kTp"]
    vo_i = io["vo"]; vp_i = io["vp"]; xs = io["xs"]
    WoT = io["WoT"]; msc = io["msc"]
    x1_o = io["x1"]; xlnT_o = io["xlnT"]
    mem = Mem(nc, ctx)
    ident, b_id = emit_identity(tk, mem)
    st = mk_stats(mem)
    st["epln"] = const_col(tk, mem, "epln", EPS_LN)

    QT, bQT = mem.sb("QT", [128, 16384], BF16)
    tk.emit("sp", lambda q: q.dma_start(
        out=QT.rearrange("p (r c) -> p r c", r=qT_i.shape[0]),
        in_=qT_i.rearrange("r p c -> p r c")), writes=[bQT], dma=pfx + "qt")
    if layer == 0:
        KTO, bKTO = mem.sb("KTO", [128, NT * 1024], BF16)
        KTP, bKTP = mem.sb("KTP", [128, NT * 1024], BF16)
        tk.emit("sp", lambda q: q.dma_start(
            out=KTO.rearrange("p (r c) -> p r c", r=NT),
            in_=kTo_i.rearrange("r p c -> p r c")), writes=[bKTO], dma=pfx + "kto")
        tk.emit("sp", lambda q: q.dma_start(
            out=KTP.rearrange("p (r c) -> p r c", r=NT),
            in_=kTp_i.rearrange("r p c -> p r c")), writes=[bKTP], dma=pfx + "ktp")
    else:
        KH = [mem.sb(f"KH{i}", [128, 2048], BF16) for i in range(2)]
    VO, bVO = mem.sb("VO", [128, NT * nk * 128], BF16)
    VP, bVP = mem.sb("VP", [128, NT * nk * 128], BF16)
    tk.emit("sp", lambda q: q.dma_start(
        out=VO.rearrange("p (r c) -> p r c", r=NT),
        in_=vo_i.rearrange("r p c -> p r c")), writes=[bVO], dma=pfx + "vo")
    tk.emit("sp", lambda q: q.dma_start(
        out=VP.rearrange("p (r c) -> p r c", r=NT),
        in_=vp_i.rearrange("r p c -> p r c")), writes=[bVP], dma=pfx + "vp")

    MS, bMS = mem.sb("MS", [128, 2], F32)
    tk.emit("sp", lambda q: q.dma_start(out=MS, in_=msc), writes=[bMS], dma=pfx + "ms")
    TRI, bTRI = mem.sb("TRI", [128, 128], F32)
    tk.emit("pl", lambda q: q.memset(TRI, 0.0), writes=[bTRI])
    tk.emit("pl", lambda q: q.affine_select(
        out=TRI, in_=TRI, compare_op=AluOpType.is_ge, fill=NEG,
        base=0, pattern=[[-1, 128]], channel_multiplier=1),
        reads=[bTRI], writes=[bTRI])
    if layer == 0:
        EDG, bEDG = mem.sb("EDG", [128, 128], F32)
        tk.emit("pl", lambda q: q.memset(EDG, 0.0), writes=[bEDG])
        tk.emit("pl", lambda q: q.affine_select(
            out=EDG, in_=EDG, compare_op=AluOpType.is_ge, fill=NEG,
            base=-1, pattern=[[1, 128]], channel_multiplier=-1),
            reads=[bEDG], writes=[bEDG])

    ET, bET = mem.sb("ET", [128, 16 * 1024], BF16)
    ENC, bENC = mem.sb("ENC", [128, 16 * 1024], BF16)
    SSp = [mem.sb(f"SS{i}", [128, 2048], F32) for i in range(2)]
    EE, bEE = mem.sb("EE", [128, 2048], BF16)
    EFp = [mem.sb(f"EF{i}", [128, 2048], BF16) for i in range(2)]
    EF, bEF = EFp[0]
    nmp = [mem.sb(f"nmv{i}", [128, 1], F32) for i in range(2)]
    lp = [mem.sb(f"lv{i}", [128, 1], F32) for i in range(2)]
    rlp = [mem.sb(f"rlv{i}", [128, 1], F32) for i in range(2)]

    S4, _ = mem.ps("S4", [128, 2048], F32)
    bS4 = [Buf(f"S4_{i}") for i in range(4)]
    psT = [mem.ps(f"psT{i}", [128, 1024], BF16) for i in range(2)]
    psE = [mem.ps(f"psE{i}", [128, 512], F32) for i in range(2)]

    def qslice(n, r):
        if layer == 0:
            return QT[:, r * 2048 + n * 128: r * 2048 + (n + 1) * 128]
        return QT[:, n * 1024 + r * 128: n * 1024 + (r + 1) * 128]

    ETv = ET.rearrange("p (s c) -> p s c", s=16)

    for n in range(16):
        kv = n // 2 if layer == 0 else n
        if layer == 1:
            KHo, bKHo = KH[0]
            KHp, bKHp = KH[1]
            tk.emit("sp", lambda q, n=n, KHo=KHo: q.dma_start(
                out=KHo[:, 0:1024], in_=kTo_i[n]), writes=[bKHo], dma=pfx + "kh0")
            tk.emit("sp", lambda q, n=n, KHp=KHp: q.dma_start(
                out=KHp[:, 0:1024], in_=kTp_i[n]), writes=[bKHp], dma=pfx + "kh1")
        def do_scores(r):
            nb = wdt_of(layer, r)
            p0 = r - nb + 1
            w = nb * 128
            SS, bSS = SSp[r % 2]
            EF, bEF = EFp[r % 2]
            nmv, bnm = nmp[r % 2]
            lv, bl = lp[r % 2]
            rlv, brl = rlp[r % 2]
            lhs = qslice(n, r)
            for (side, base_col) in ((0, 0), (1, 1024)):
                if layer == 0:
                    KT = KTO if side == 0 else KTP
                    bKT = bKTO if side == 0 else bKTP
                    ktv = KT.rearrange("p (s c) -> p s c", s=NT)

                    def g(q, ktv=ktv, p0=p0, nb=nb, kv=kv, lhs=lhs,
                          base_col=base_col):
                        last = None
                        done = 0
                        while done < nb:
                            m = min(4, nb - done)
                            rhs = ktv[:, p0 + done:p0 + done + m,
                                      kv * 128:(kv + 1) * 128]
                            last = q.matmul(
                                S4[:, base_col + done * 128:
                                   base_col + (done + m) * 128],
                                lhs, rhs, start=True, stop=True)
                            done += m
                        return last
                    rbufs = [bQT, bKT]
                else:
                    KHa, bKHa = KH[side]

                    def g(q, KHa=KHa, p0=p0, nb=nb, lhs=lhs,
                          base_col=base_col):
                        last = None
                        done = 0
                        while done < nb:
                            m = min(4, nb - done)
                            rhs = KHa[:, (p0 + done) * 128:
                                      (p0 + done + m) * 128]
                            last = q.matmul(
                                S4[:, base_col + done * 128:
                                   base_col + (done + m) * 128],
                                lhs, rhs, start=True, stop=True)
                            done += m
                        return last
                    rbufs = [bQT, bKHa]
                banks = [base_col // 512 + i for i in range((w + 511) // 512)]
                tk.emit("pe", g, reads=rbufs,
                        writes=[bS4[i] for i in banks])

            ob = [bS4[i] for i in range((w + 511) // 512)]
            pb2 = [bS4[2 + i] for i in range((w + 511) // 512)]
            edge = (layer == 0 and r >= 4)
            c0 = 128 if edge else 0
            if edge:
                tk.emit("dve", lambda q: q.tensor_tensor(
                    SS[:, 0:128], S4[:, 0:128], EDG, op=AluOpType.add),
                    reads=ob + [bEDG], writes=[bSS])
            if w - 128 > c0:
                tk.emit("dve", lambda q, c0=c0, w=w: q.tensor_copy(
                    SS[:, c0:w - 128], S4[:, c0:w - 128]),
                    reads=ob, writes=[bSS])
            tk.emit("dve", lambda q, w=w: q.tensor_tensor(
                SS[:, w - 128:w], S4[:, w - 128:w], TRI, op=AluOpType.add),
                reads=ob + [bTRI], writes=[bSS])
            if edge:
                tk.emit("dve", lambda q: q.tensor_scalar(
                    SS[:, 1024:1152], S4[:, 1024:1152], MS[:, 1:2], None,
                    op0=AluOpType.add), reads=pb2 + [bMS], writes=[bSS])
            if w - 128 > c0:
                tk.emit("dve", lambda q, c0=c0, w=w: q.tensor_copy(
                    SS[:, 1024 + c0:1024 + w - 128],
                    S4[:, 1024 + c0:1024 + w - 128]),
                    reads=pb2, writes=[bSS])
            tk.emit("dve", lambda q, w=w: q.tensor_scalar(
                SS[:, 1024 + w - 128:1024 + w], S4[:, 1024 + w - 128:1024 + w],
                MS[:, 0:1], None, op0=AluOpType.add),
                reads=pb2 + [bMS], writes=[bSS])

            sv = SS.rearrange("p (s c) -> p s c", s=2)[:, :, 0:w]
            ev = EE.rearrange("p (s c) -> p s c", s=2)[:, :, 0:w]
            fv = EF.rearrange("p (s c) -> p s c", s=2)[:, :, 0:w]
            tk.emit("dve", lambda q, sv=sv, nmv=nmv: q.reduce_max(
                nmv, sv, axis=AX.XY, negate=True), reads=[bSS], writes=[bnm])
            tk.emit("act", lambda q, sv=sv, ev=ev, nmv=nmv, lv=lv: q.activation(
                ev, sv, AF.Exp, bias=nmv, accum_out=lv),
                reads=[bSS, bnm], writes=[bEE, bl])
            tk.emit("dve", lambda q, rlv=rlv, lv=lv: q.reciprocal(rlv, lv),
                    reads=[bl], writes=[brl])
            tk.emit("act", lambda q, ev=ev, fv=fv, rlv=rlv: q.activation(
                fv, ev, AF.Identity, scale=rlv),
                reads=[bEE, brl], writes=[bEF])

        def do_ET(r):
            nb = wdt_of(layer, r)
            p0 = r - nb + 1
            EF, bEF = EFp[r % 2]
            for (side, base_col, slot0) in ((0, 0, p0), (1, 1024, 8 + p0)):
                pb, bpb = psT[side]

                def tg(q, pb=pb, base_col=base_col, nb=nb, EF=EF):
                    last = None
                    for i in range(nb):
                        last = q.matmul(
                            pb[:, i * 128:(i + 1) * 128],
                            EF[:, base_col + i * 128:base_col + (i + 1) * 128],
                            ident, start=True, stop=True,
                            is_transpose=True, skip_group_check=True)
                    return last
                tk.emit("pe", tg, reads=[bEF, b_id], writes=[bpb])
                dstv = ETv[:, slot0:slot0 + nb, r * 128:(r + 1) * 128]
                tk.emit("dve", lambda q, pb=pb, dstv=dstv, nb=nb:
                        q.tensor_copy(dstv, pb[:, 0:nb * 128]),
                        reads=[bpb], writes=[bET])

        for r in range(NT):
            do_scores(r)
            if r >= 1:
                do_ET(r - 1)
        do_ET(NT - 1)

        vov = VO.rearrange("p (s c) -> p s c", s=NT)
        vpv = VP.rearrange("p (s c) -> p s c", s=NT)
        for half in range(2):
            t0h, t1h = half * 512, (half + 1) * 512
            pieces = []
            for side in range(2):
                for p in range(NT):
                    ta = p * 128
                    tb = T if layer == 1 else min(T, (p + 5) * 128)
                    a, b2 = max(ta, t0h), min(tb, t1h)
                    if a < b2:
                        pieces.append((side, p, a, b2))
            pieces.sort(key=lambda t: (t[2] - t0h != 0 or t[3] - t1h != 0))
            assert pieces[0][2] == t0h and pieces[0][3] == t1h, pieces[:2]

            def g(q, pieces=pieces, t0h=t0h, half=half, kv=kv):
                last = None
                for idx, (side, p, a, b2) in enumerate(pieces):
                    vsl = (vov if side == 0 else vpv)[
                        :, p, kv * 128:(kv + 1) * 128]
                    slot = p + 8 * side
                    rhs = ET[:, slot * 1024 + a:slot * 1024 + b2]
                    last = q.matmul(
                        psE[half][0][:, a - t0h:b2 - t0h], vsl, rhs,
                        start=(idx == 0), stop=(idx == len(pieces) - 1),
                        skip_group_check=True)
                return last
            tk.emit("pe", g, reads=[bET, bVO, bVP], writes=[psE[half][1]])
            tk.emit("dve", lambda q, half=half, n=n: q.tensor_copy(
                ENC[:, n * 1024 + half * 512:n * 1024 + (half + 1) * 512],
                psE[half][0]), reads=[psE[half][1]], writes=[bENC])

    bX1 = Buf("x1hbm")
    WOS = [mem.sb(f"WOS{i}", [128, 512], BF16) for i in range(2)]
    OSTG = [mem.sb(f"OSTG{i}", [128, 512], F32) for i in range(1)]
    OSTG.append(OSTG[0])
    for gidx in range(2):
        tts = [gidx * 4 + i for i in range(4)]
        for ds in range(4):
            for fc in range(16):
                Wa, bWa = WOS[fc % 2]
                tk.emit("pl", lambda q, fc=fc, ds=ds, Wa=Wa: q.dma_start(
                    out=Wa, in_=WoT[fc * 128:(fc + 1) * 128,
                                    ds * 512:(ds + 1) * 512]),
                    writes=[bWa], dma=pfx + "wo%d" % (fc % 2))

                def g(q, fc=fc, Wa=Wa, tts=tts):
                    last = None
                    for i, tt in enumerate(tts):
                        last = q.matmul(
                            S4[:, i * 512:(i + 1) * 512],
                            ENC[:, fc * 128 + tt * 0:0] if False else
                            ENC[:, fc * 1024 + tt * 128:
                                fc * 1024 + (tt + 1) * 128],
                            Wa, start=(fc == 0), stop=(fc == 15),
                            skip_group_check=True)
                    return last
                tk.emit("pe", g, reads=[bENC, bWa],
                        writes=[bS4[i] for i in range(4)])
            for i, tt in enumerate(tts):
                Oa, bOa = OSTG[i % 2]
                tk.emit("dve", lambda q, i=i, Oa=Oa: q.tensor_copy(
                    Oa, S4[:, i * 512:(i + 1) * 512]),
                    reads=[bS4[i]], writes=[bOa])
                tk.emit("sp", lambda q, tt=tt, ds=ds, Oa=Oa: q.dma_start(
                    out=x1_o[tt * 128:(tt + 1) * 128, ds * 512:(ds + 1) * 512],
                    in_=Oa), reads=[bOa], writes=[bX1], dma=pfx + "os%d" % (i % 2))

    XLN1, bXLN1 = ENC[:, 0:D], bENC
    J1B = ENC[:, D:2 * D]
    XST, bXST = EE, bEE
    PART = ET.bitcast(F32)[:, 0:D]
    X1T = ET.bitcast(F32)[:, D:2 * D]
    X = [(ET.bitcast(F32)[:, 2 * D:3 * D], bET)]
    for r in range(NT):
        Xa, bXa = X[0]
        tk.emit("sp", lambda q, r=r: q.dma_start(
            out=PART, in_=x1_o[r * 128:(r + 1) * 128, :]),
            reads=[bX1], writes=[bET], dma=pfx + "x3p")
        tk.emit("sp", lambda q, r=r, Xa=Xa: q.dma_start(
            out=Xa, in_=xs[r * 128:(r + 1) * 128, :]),
            writes=[bXa], dma=pfx + "x3x")
        tk.emit("dve", lambda q, Xa=Xa: q.tensor_tensor(
            X1T, Xa, PART, op=AluOpType.add),
            reads=[bXa, bET], writes=[bET])
        tk.emit("sp", lambda q, r=r: q.dma_start(
            out=x1_o[r * 128:(r + 1) * 128, :], in_=X1T),
            reads=[bET], writes=[bX1], dma=pfx + "x1w")
        emit_ln(tk, st, X1T, bET, XLN1, bXLN1, (J1B, bENC), SSp[0])
        emit_tr16(tk, ident, b_id, XLN1, bXLN1, XST, bXST, 16, psT)
        tk.emit("sp", lambda q, r=r: q.dma_start(
            out=xlnT_o[r], in_=XST), reads=[bXST], dma=pfx + "xst")

    assert mem.bpp <= 207000, f"SBUF over budget: {mem.bpp}"


def build_B(layer):
    nc = bass.Bass("TRN2", target_bir_lowering=False)
    nk = 8 if layer == 0 else 16
    io = {}
    if layer == 0:
        io["qT"] = nc.dram_tensor("qT", [NT, 128, 2048], BF16, kind="ExternalInput").ap()
        io["kTo"] = nc.dram_tensor("kTo", [NT, 128, nk * 128], BF16, kind="ExternalInput").ap()
        io["kTp"] = nc.dram_tensor("kTp", [NT, 128, nk * 128], BF16, kind="ExternalInput").ap()
    else:
        io["qT"] = nc.dram_tensor("qT", [16, 128, T], BF16, kind="ExternalInput").ap()
        io["kTo"] = nc.dram_tensor("kTo", [nk, 128, T], BF16, kind="ExternalInput").ap()
        io["kTp"] = nc.dram_tensor("kTp", [nk, 128, T], BF16, kind="ExternalInput").ap()
    io["vo"] = nc.dram_tensor("vo", [NT, 128, nk * 128], BF16, kind="ExternalInput").ap()
    io["vp"] = nc.dram_tensor("vp", [NT, 128, nk * 128], BF16, kind="ExternalInput").ap()
    io["xs"] = nc.dram_tensor("xs", [T, D], F32, kind="ExternalInput").ap()
    io["WoT"] = nc.dram_tensor("WoT", [2048, D], F32, kind="ExternalInput").ap()
    io["msc"] = nc.dram_tensor("msc", [128, 2], F32, kind="ExternalInput").ap()
    io["x1"] = nc.dram_tensor("x1", [T, D], F32, kind="ExternalOutput").ap()
    io["xlnT"] = nc.dram_tensor("xlnT", [NT, 128, 2048], BF16, kind="ExternalOutput").ap()
    with ExitStack() as gctx:
        tk = TK(nc, gctx)
        with ExitStack() as ctx:
            emit_B(nc, tk, ctx, layer, io)
        tk.finish()
    return nc




def host_prep_B(layer, inputs, qT, kT, v, xs_by_core):
    nc = build_B(layer)
    if layer == 0:
        WoT = np.ascontiguousarray(
            inputs["L0_attn_out_kernel"].reshape(2048, D))
    else:
        WoT = np.ascontiguousarray(inputs["L1_attn_W_O"].T)
    in_maps = []
    for c in range(8):
        h = c % 2
        even, odd = c - h, c - h + 1
        own, par = (even, odd) if h == 0 else (odd, even)
        m0 = NEG if h == 0 else 0.0
        m1 = 0.0 if h == 0 else NEG
        msc = np.zeros((128, 2), np.float32)
        msc[:, 0] = m0
        msc[:, 1] = m1
        in_maps.append({
            "qT": qT[c], "kTo": kT[own], "kTp": kT[par],
            "vo": v[own], "vp": v[par], "xs": xs_by_core[c],
            "WoT": WoT, "msc": msc})
    return nc, in_maps





def emit_C0(nc, tk, ctx, half, io, pfx=""):
    xlnT_i = io["xlnT"]; Wg = io["Wg"]; Wu = io["Wu"]; Wd = io["Wd"]
    out_o = io["out"]
    if half == 1:
        part_i = io["part"]; x1_i = io["x1"]
    mem = Mem(nc, ctx)
    ident, b_id = emit_identity(tk, mem)

    XLT, bXLT = mem.sb("XLT", [128, 16384], BF16)
    tk.emit("sp", lambda q: q.dma_start(
        out=XLT.rearrange("p (r c) -> p r c", r=NT),
        in_=xlnT_i.rearrange("r p c -> p r c")), writes=[bXLT], dma=pfx + "xlt")

    H = [mem.sb(f"H{r}", [128, 4096], BF16) for r in range(NT)]
    HS, bHS = mem.sb("HS", [128, 4096], BF16)
    WGs = [mem.sb(f"WGs{i}", [128, 16 * 512], BF16) for i in range(2)]
    WUs = [mem.sb(f"WUs{i}", [128, 16 * 512], BF16) for i in range(2)]
    WDs = [mem.sb(f"WDs{i}", [128, 512], BF16) for i in range(2)]
    GSc = [mem.sb(f"GS{i}", [128, 512], F32) for i in range(2)]
    GS, bGS = GSc[0]
    SCR = [mem.sb(f"SCR{i}", [128, 1024], BF16) for i in range(6)]
    OST = [mem.sb(f"OST{i}", [128, 512], F32) for i in range(2)]
    AH, bAH = mem.sb("AH", [128, 4096], BF16)

    psD = [mem.ps(f"psD{i}", [128, 512], F32) for i in range(6)]
    psT = [mem.ps(f"psTt{i}", [128, 1024], BF16) for i in range(2)]

    for s in range(8):
        WGa, bWGa = WGs[s % 2]
        WUa, bWUa = WUs[s % 2]
        tk.emit("pl", lambda q, s=s, WGa=WGa: q.dma_start(
            out=WGa.rearrange("p (c i) -> p c i", c=16),
            in_=Wg[:, s * 512:(s + 1) * 512].rearrange(
                "(c p) i -> p c i", p=128)),
            writes=[bWGa], dma=pfx + "wg%d" % (s % 2))
        tk.emit("pl", lambda q, s=s, WUa=WUa: q.dma_start(
            out=WUa.rearrange("p (c i) -> p c i", c=16),
            in_=Wu[:, s * 512:(s + 1) * 512].rearrange(
                "(c p) i -> p c i", p=128)),
            writes=[bWUa], dma=pfx + "wu%d" % (s % 2))
        for r in range(NT):
            pg, bpg = psD[(2 * r) % 4]
            pu, bpu = psD[(2 * r) % 4 + 1]

            def gmm(q, W, pg, r):
                last = None
                for c in range(16):
                    last = q.matmul(
                        pg, XLT[:, r * 2048 + c * 128:r * 2048 + (c + 1) * 128],
                        W[:, c * 512:(c + 1) * 512],
                        start=(c == 0), stop=(c == 15))
                return last
            tk.emit("pe", lambda q, WGa=WGa, pg=pg, r=r: gmm(q, WGa, pg, r),
                    reads=[bXLT, bWGa], writes=[bpg])
            tk.emit("pe", lambda q, WUa=WUa, pu=pu, r=r: gmm(q, WUa, pu, r),
                    reads=[bXLT, bWUa], writes=[bpu])
            GSa, bGSa = GSc[r % 2]
            tk.emit("act", lambda q, pg=pg, GSa=GSa: q.activation(
                GSa, pg, AF.Silu), reads=[bpg], writes=[bGSa])
            Ha, bHa = H[r]
            hv = Ha[:, s * 512:(s + 1) * 512].rearrange("p (m g) -> p g m", m=4)
            tk.emit("dve", lambda q, pu=pu, hv=hv, GSa=GSa: q.tensor_tensor(
                hv, GSa, pu, op=AluOpType.mult),
                reads=[bGSa, bpu], writes=[bHa])

    for r in range(NT):
        Ha, bHa = H[r]
        h4 = Ha.rearrange("p (s m g) -> p s m g", s=8, m=4)
        a4 = AH.rearrange("p (s m g) -> p s m g", s=8, m=4)
        tk.emit("act", lambda q, Ha=Ha: q.activation(AH, Ha, AF.Abs),
                reads=[bHa], writes=[bAH])
        C = {}
        bC = {}
        pairs = [(0, 1), (0, 2), (0, 3), (1, 2), (1, 3), (2, 3)]
        for idx, (i, j) in enumerate(pairs):
            Cc, bCc = SCR[idx]
            tk.emit("dve", lambda q, Cc=Cc, i=i, j=j, a4=a4: q.tensor_tensor(
                Cc.rearrange("p (s g) -> p s g", s=8),
                a4[:, :, i:i + 1, :].squeeze(2),
                a4[:, :, j:j + 1, :].squeeze(2), op=AluOpType.is_ge),
                reads=[bAH], writes=[bCc])
            C[(i, j)] = Cc
            bC[(i, j)] = bCc
        U, V, W2t, X3t = (AH[:, 0:1024], AH[:, 1024:2048],
                          AH[:, 2048:3072], AH[:, 3072:4096])
        bU = bV = bW2t = bX3t = bAH
        tk.emit("dve", lambda q: q.tensor_tensor(
            U, C[(0, 1)], C[(0, 2)], op=AluOpType.add),
            reads=[bC[(0, 1)], bC[(0, 2)]], writes=[bU])
        tk.emit("dve", lambda q: q.tensor_tensor(
            U, U, C[(0, 3)], op=AluOpType.add),
            reads=[bU, bC[(0, 3)]], writes=[bU])
        tk.emit("dve", lambda q: q.tensor_tensor(
            V, C[(1, 2)], C[(1, 3)], op=AluOpType.add),
            reads=[bC[(1, 2)], bC[(1, 3)]], writes=[bV])
        tk.emit("dve", lambda q: q.tensor_tensor(
            V, V, C[(0, 1)], op=AluOpType.subtract),
            reads=[bV, bC[(0, 1)]], writes=[bV])
        tk.emit("dve", lambda q: q.tensor_tensor(
            W2t, C[(2, 3)], C[(0, 2)], op=AluOpType.subtract),
            reads=[bC[(2, 3)], bC[(0, 2)]], writes=[bW2t])
        tk.emit("dve", lambda q: q.tensor_tensor(
            W2t, W2t, C[(1, 2)], op=AluOpType.subtract),
            reads=[bW2t, bC[(1, 2)]], writes=[bW2t])
        tk.emit("dve", lambda q: q.tensor_tensor(
            X3t, C[(0, 3)], C[(1, 3)], op=AluOpType.add),
            reads=[bC[(0, 3)], bC[(1, 3)]], writes=[bX3t])
        tk.emit("dve", lambda q: q.tensor_tensor(
            X3t, X3t, C[(2, 3)], op=AluOpType.add),
            reads=[bX3t, bC[(2, 3)]], writes=[bX3t])
        hs4 = HS.rearrange("p (s m g) -> p s m g", s=8, m=4)
        thr = [(U, bU, 1.5, AluOpType.is_ge), (V, bV, 0.5, AluOpType.is_ge),
               (W2t, bW2t, -0.5, AluOpType.is_ge), (X3t, bX3t, 1.5, AluOpType.is_le)]
        for m in range(4):
            Bt, bBt, tv, opx = thr[m]
            tk.emit("dve", lambda q, Bt=Bt, tv=tv, opx=opx, m=m, h4=h4, hs4=hs4:
                    q.scalar_tensor_tensor(
                hs4[:, :, m:m + 1, :].squeeze(2),
                Bt.rearrange("p (s g) -> p s g", s=8), tv,
                h4[:, :, m:m + 1, :].squeeze(2),
                op0=opx, op1=AluOpType.mult),
                reads=[bBt, bHa], writes=[bHS])
        emit_tr16(tk, ident, b_id, HS, bHS, Ha, bHa, 32, psT)

    if half == 1:
        PIN = XLT.bitcast(F32)[:, 0:D]
        X1b = XLT.bitcast(F32)[:, D:2 * D]
    for grp in ([0, 1, 2, 3, 4, 5], [6, 7]):
        for ds in range(4):
            for cc in range(32):
                WDa, bWDa = WDs[cc % 2]
                tk.emit("pl", lambda q, cc=cc, ds=ds, WDa=WDa: q.dma_start(
                    out=WDa, in_=Wd[cc * 128:(cc + 1) * 128,
                                    ds * 512:(ds + 1) * 512]),
                    writes=[bWDa], dma=pfx + "wd%d" % (cc % 2))

                def g(q, cc=cc, WDa=WDa, grp=grp):
                    last = None
                    for gi, r in enumerate(grp):
                        last = q.matmul(
                            psD[gi][0],
                            H[r][0][:, cc * 128:(cc + 1) * 128], WDa,
                            start=(cc == 0), stop=(cc == 31),
                            skip_group_check=True)
                    return last
                tk.emit("pe", g, reads=[bWDa] + [H[r][1] for r in grp],
                        writes=[psD[gi][1] for gi in range(len(grp))])
            for gi, r in enumerate(grp):
                Oa, bOa = OST[gi % 2]
                if half == 0:
                    tk.emit("dve", lambda q, gi=gi, Oa=Oa: q.tensor_copy(
                        Oa, psD[gi][0]), reads=[psD[gi][1]], writes=[bOa])
                else:
                    tk.emit("sp", lambda q, r=r, ds=ds: q.dma_start(
                        out=PIN[:, 0:512], in_=part_i[
                            r * 128:(r + 1) * 128, ds * 512:(ds + 1) * 512]),
                        writes=[bXLT], dma=pfx + "pin")
                    tk.emit("sp", lambda q, r=r, ds=ds: q.dma_start(
                        out=X1b[:, 0:512], in_=x1_i[
                            r * 128:(r + 1) * 128, ds * 512:(ds + 1) * 512]),
                        writes=[bXLT], dma=pfx + "pin")
                    tk.emit("dve", lambda q, gi=gi: q.tensor_tensor(
                        GS, psD[gi][0], PIN[:, 0:512], op=AluOpType.add),
                        reads=[psD[gi][1], bXLT], writes=[bGS])
                    tk.emit("dve", lambda q, Oa=Oa: q.tensor_tensor(
                        Oa, GS, X1b[:, 0:512], op=AluOpType.add),
                        reads=[bGS, bXLT], writes=[bOa])
                tk.emit("sp", lambda q, r=r, ds=ds, Oa=Oa: q.dma_start(
                    out=out_o[r * 128:(r + 1) * 128, ds * 512:(ds + 1) * 512],
                    in_=Oa), reads=[bOa], dma=pfx + "ost%d" % (gi % 2))

    assert mem.bpp <= 207000, f"SBUF over budget: {mem.bpp}"


def build_C0(half):
    nc = bass.Bass("TRN2", target_bir_lowering=False)
    io = {}
    io["xlnT"] = nc.dram_tensor("xlnT", [NT, 128, 2048], BF16, kind="ExternalInput").ap()
    io["Wg"] = nc.dram_tensor("Wg", [D, 4096], F32, kind="ExternalInput").ap()
    io["Wu"] = nc.dram_tensor("Wu", [D, 4096], F32, kind="ExternalInput").ap()
    io["Wd"] = nc.dram_tensor("Wd", [4096, D], F32, kind="ExternalInput").ap()
    if half == 1:
        io["part"] = nc.dram_tensor("part", [T, D], F32, kind="ExternalInput").ap()
        io["x1"] = nc.dram_tensor("x1", [T, D], F32, kind="ExternalInput").ap()
    io["out"] = nc.dram_tensor("out", [T, D], F32, kind="ExternalOutput").ap()
    with ExitStack() as gctx:
        tk = TK(nc, gctx)
        with ExitStack() as ctx:
            emit_C0(nc, tk, ctx, half, io)
        tk.finish()
    return nc




def emit_C1(nc, tk, ctx, io, pfx=""):
    xlnT_i = io["xlnT"]; W13 = io["W13"]; W2T = io["W2T"]
    H128i = io["H128"]; x1_i = io["x1"]; out_o = io["out"]
    NC = 64
    mem = Mem(nc, ctx)

    XLT, bXLT = mem.sb("XLT", [128, 16384], BF16)
    tk.emit("sp", lambda q: q.dma_start(
        out=XLT.rearrange("p (r c) -> p r c", r=NT),
        in_=xlnT_i.rearrange("r p c -> p r c")), writes=[bXLT], dma=pfx + "xlt")
    H128S, bH128 = mem.sb("H128S", [128, 128], BF16)
    tk.emit("pl", lambda q: q.dma_start(out=H128S, in_=H128i),
            writes=[bH128], dma=pfx + "h128")

    HF, _ = mem.sb("HF", [128, 66 * 1024], BF16)
    bslot = [Buf(f"hf{i}") for i in range(66)]
    slot_of = list(range(NC))
    free_slots = [64, 65]

    W13s = [mem.sb(f"W13s{i}", [128, 16 * 256], BF16) for i in range(2)]
    W2s = [mem.sb(f"W2s{i}", [128, 512], BF16) for i in range(2)]
    GSp = [mem.sb(f"GS{i}", [128, 512], F32) for i in range(2)]
    OST = [mem.sb(f"OST{i}", [128, 512], F32) for i in range(2)]
    PIN = [mem.sb(f"PIN{i}", [128, 512], F32) for i in range(2)]
    psD = [mem.ps(f"psD{i}", [128, 512], F32) for i in range(8)]
    xv = XLT.rearrange("p (r c t) -> p c r t", c=16, t=128)

    def bfly(a, b2):
        sa, sb = slot_of[a], slot_of[b2]
        s1, s2 = free_slots[0], free_slots[1]
        A = HF[:, sa * 1024:(sa + 1) * 1024]
        B = HF[:, sb * 1024:(sb + 1) * 1024]
        U = HF[:, s1 * 1024:(s1 + 1) * 1024]
        V = HF[:, s2 * 1024:(s2 + 1) * 1024]
        tk.emit("dve", lambda q, A=A, B=B, U=U: q.tensor_tensor(
            U, A, B, op=AluOpType.add),
            reads=[bslot[sa], bslot[sb]], writes=[bslot[s1]])
        tk.emit("dve", lambda q, A=A, B=B, V=V: q.tensor_tensor(
            V, A, B, op=AluOpType.subtract),
            reads=[bslot[sa], bslot[sb]], writes=[bslot[s2]])
        slot_of[a], slot_of[b2] = s1, s2
        free_slots[0], free_slots[1] = sa, sb

    pend = []

    def do_h128(c):
        slot = slot_of[c]
        hc = HF[:, slot * 1024:(slot + 1) * 1024]
        for m in range(2):
            ph, bph = psD[4 + m]
            tk.emit("pe", lambda q, ph=ph, hc=hc, m=m: q.matmul(
                ph, H128S, hc[:, m * 512:(m + 1) * 512],
                start=True, stop=True),
                reads=[bH128, bslot[slot]], writes=[bph])
            tk.emit("act", lambda q, ph=ph, hc=hc, m=m: q.activation(
                hc[:, m * 512:(m + 1) * 512], ph, AF.Copy,
                scale=2.0 ** -6.5), reads=[bph], writes=[bslot[slot]])
        for s in range(6):
            blk = 1 << (s + 1)
            if (c + 1) % blk == 0:
                step = 1 << s
                for a in range(c + 1 - blk, c + 1):
                    if not (a & step):
                        bfly(a, a | step)

    for c in range(NC):
        Wa, bWa = W13s[c % 2]
        tk.emit("pl", lambda q, c=c, Wa=Wa: q.dma_start(
            out=Wa.rearrange("p (k x) -> p k x", k=16),
            in_=W13[c].rearrange("(k p) x -> p k x", p=128)),
            writes=[bWa], dma=pfx + "w13_%d" % (c % 2))
        slot = slot_of[c]
        hc = HF[:, slot * 1024:(slot + 1) * 1024]
        for m in range(2):
            pg, bpg = psD[2 * (c % 2)]
            pu, bpu = psD[2 * (c % 2) + 1]
            GS, bGS = GSp[m]

            def gmm(q, which, pb, m, Wa):
                last = None
                for k in range(16):
                    last = q.matmul(
                        pb, Wa[:, k * 256 + which * 128:
                               k * 256 + (which + 1) * 128],
                        xv[:, k, m * 4:(m + 1) * 4, :],
                        start=(k == 0), stop=(k == 15))
                return last
            tk.emit("pe", lambda q, pg=pg, m=m, Wa=Wa: gmm(q, 0, pg, m, Wa),
                    reads=[bXLT, bWa], writes=[bpg])
            tk.emit("pe", lambda q, pu=pu, m=m, Wa=Wa: gmm(q, 1, pu, m, Wa),
                    reads=[bXLT, bWa], writes=[bpu])
            tk.emit("act", lambda q, pg=pg, GS=GS: q.activation(GS, pg, AF.Silu),
                    reads=[bpg], writes=[bGS])
            tk.emit("dve", lambda q, pu=pu, hc=hc, m=m, GS=GS: q.tensor_tensor(
                hc[:, m * 512:(m + 1) * 512], GS, pu, op=AluOpType.mult),
                reads=[bGS, bpu], writes=[bslot[slot]])
        if c >= 1:
            do_h128(c - 1)
    do_h128(NC - 1)

    for ds in range(4):
        for cc in range(NC):
            Wa, bWa = W2s[cc % 2]
            tk.emit("pl", lambda q, cc=cc, ds=ds, Wa=Wa: q.dma_start(
                out=Wa, in_=W2T[cc * 128:(cc + 1) * 128,
                                ds * 512:(ds + 1) * 512]),
                writes=[bWa], dma=pfx + "w2_%d" % (cc % 2))
            slot = slot_of[cc]

            def g(q, slot=slot, Wa=Wa, cc=cc):
                last = None
                for r in range(NT):
                    last = q.matmul(
                        psD[r][0],
                        HF[:, slot * 1024 + r * 128:slot * 1024 + (r + 1) * 128],
                        Wa, start=(cc == 0), stop=(cc == NC - 1),
                        skip_group_check=True)
                return last
            tk.emit("pe", g, reads=[bWa, bslot[slot]],
                    writes=[psD[r][1] for r in range(NT)])
        for r in range(NT):
            Pa, bPa = PIN[r % 2]
            Oa, bOa = OST[r % 2]
            tk.emit("sp", lambda q, r=r, ds=ds, Pa=Pa: q.dma_start(
                out=Pa, in_=x1_i[r * 128:(r + 1) * 128,
                                 ds * 512:(ds + 1) * 512]),
                writes=[bPa], dma=pfx + "pin%d" % (r % 2))
            tk.emit("dve", lambda q, r=r, Oa=Oa, Pa=Pa: q.tensor_tensor(
                Oa, psD[r][0], Pa, op=AluOpType.add),
                reads=[psD[r][1], bPa], writes=[bOa])
            tk.emit("sp", lambda q, r=r, ds=ds, Oa=Oa: q.dma_start(
                out=out_o[r * 128:(r + 1) * 128, ds * 512:(ds + 1) * 512],
                in_=Oa), reads=[bOa], dma=pfx + "ost%d" % (r % 2))

    assert mem.bpp <= 207000, f"SBUF over budget: {mem.bpp}"


def build_C1():
    nc = bass.Bass("TRN2", target_bir_lowering=False)
    io = {}
    io["xlnT"] = nc.dram_tensor("xlnT", [NT, 128, 2048], BF16, kind="ExternalInput").ap()
    io["W13"] = nc.dram_tensor("W13", [64, D, 256], F32, kind="ExternalInput").ap()
    io["W2T"] = nc.dram_tensor("W2T", [8192, D], F32, kind="ExternalInput").ap()
    io["H128"] = nc.dram_tensor("H128", [128, 128], F32, kind="ExternalInput").ap()
    io["x1"] = nc.dram_tensor("x1", [T, D], F32, kind="ExternalInput").ap()
    io["out"] = nc.dram_tensor("out", [T, D], F32, kind="ExternalOutput").ap()
    with ExitStack() as gctx:
        tk = TK(nc, gctx)
        with ExitStack() as ctx:
            emit_C1(nc, tk, ctx, io)
        tk.finish()
    return nc




def _hadamard128():
    i = np.arange(128)
    pc = np.zeros((128, 128), np.int64)
    x = i[:, None] & i[None, :]
    while x.any():
        pc += x & 1
        x >>= 1
    return np.where(pc % 2 == 0, 1.0, -1.0).astype(np.float32)


def plane_perm_rows(wd_half):
    w = wd_half.reshape(8, 128, 4, wd_half.shape[1])
    return np.ascontiguousarray(w.transpose(0, 2, 1, 3).reshape(4096, -1))


def host_prep_C0(half, inputs, xlnT, x1=None, part=None):
    nc = build_C0(half)
    Wg = np.ascontiguousarray(inputs["L0_ffn_w_gate"][:, half * 4096:(half + 1) * 4096])
    Wu = np.ascontiguousarray(inputs["L0_ffn_w_up"][:, half * 4096:(half + 1) * 4096])
    Wd = plane_perm_rows(inputs["L0_ffn_w_down"][half * 4096:(half + 1) * 4096])
    in_maps = []
    for c in range(8):
        m = {"xlnT": xlnT[c], "Wg": Wg, "Wu": Wu, "Wd": Wd}
        if half == 1:
            m["part"] = part[c]
            m["x1"] = x1[c]
        in_maps.append(m)
    return nc, in_maps


def host_prep_C1(inputs, xlnT, x1):
    nc = build_C1()
    W1T = inputs["L1_ffn_W1"].T
    W3T = inputs["L1_ffn_W3"].T
    W13 = np.concatenate([
        W1T.reshape(D, 64, 128).transpose(1, 0, 2),
        W3T.reshape(D, 64, 128).transpose(1, 0, 2)], axis=2)
    W13 = np.ascontiguousarray(W13)
    W2T = np.ascontiguousarray(inputs["L1_ffn_W2"].T)
    H128 = _hadamard128()
    in_maps = [{"xlnT": xlnT[c], "W13": W13, "W2T": W2T, "H128": H128,
                "x1": x1[c]} for c in range(8)]
    return nc, in_maps


def kernel(**inputs):
    return kernel_fused(**inputs)


def kernel_unfused(**inputs):
    x = np.ascontiguousarray(inputs["L0_attn_x"], dtype=np.float32)
    xs = shard_x(x)
    cores = list(range(8))

    nc, maps = host_prep_A(0, inputs, xs)
    res = run_bass_kernel_spmd(nc, maps, core_ids=cores)
    qT = [res.results[c]["qT"] for c in cores]
    kT = [res.results[c]["kT"] for c in cores]
    v = [res.results[c]["v"] for c in cores]
    nc, maps = host_prep_B(0, inputs, qT, kT, v, xs)
    res = run_bass_kernel_spmd(nc, maps, core_ids=cores)
    x1 = [res.results[c]["x1"] for c in cores]
    xlnT = [res.results[c]["xlnT"] for c in cores]
    nc, maps = host_prep_C0(0, inputs, xlnT)
    res = run_bass_kernel_spmd(nc, maps, core_ids=cores)
    part = [res.results[c]["out"] for c in cores]
    nc, maps = host_prep_C0(1, inputs, xlnT, x1=x1, part=part)
    res = run_bass_kernel_spmd(nc, maps, core_ids=cores)
    xs = [res.results[c]["out"] for c in cores]

    nc, maps = host_prep_A(1, inputs, xs)
    res = run_bass_kernel_spmd(nc, maps, core_ids=cores)
    qT = [res.results[c]["qT"] for c in cores]
    kT = [res.results[c]["kT"] for c in cores]
    v = [res.results[c]["v"] for c in cores]
    nc, maps = host_prep_B(1, inputs, qT, kT, v, xs)
    res = run_bass_kernel_spmd(nc, maps, core_ids=cores)
    x1 = [res.results[c]["x1"] for c in cores]
    xlnT = [res.results[c]["xlnT"] for c in cores]
    nc, maps = host_prep_C1(inputs, xlnT, x1)
    res = run_bass_kernel_spmd(nc, maps, core_ids=cores)
    out = [res.results[c]["out"] for c in cores]
    return unshard_x(out)



RGROUPS = [[0, 1], [2, 3], [4, 5], [6, 7]]


def barrier(tk):
    for e in ENGS:
        tk.wait_all_on(e)


def emit_contrib(nc, tk, ctx, layer, io, pfx=""):
    nk = 8 if layer == 0 else 16
    mem = Mem(nc, ctx)
    kT = io["kT"]
    v = io["v"]
    rsin = io["rsin"]
    rsout = io["rsout"]
    cm = io["cmask"]
    HALF = rsin.shape[0] // 2
    KU = kT.shape[0]
    CM, bCM = mem.sb(pfx + "CM", [128, 2], F32)
    tk.emit("sp", lambda q: q.dma_start(out=CM, in_=cm), writes=[bCM],
            dma=pfx + "cm")
    TL, bTL = mem.sb(pfx + "TL", [128, 2048], BF16)
    TM, bTM = mem.sb(pfx + "TM", [128, 2048], BF16)
    b_rsin = io.setdefault("_b_rsin", Buf("rsin"))

    def unit(src_ap, rows0, width):
        tk.emit("sp", lambda q: q.dma_start(out=TL[:, 0:width], in_=src_ap),
                writes=[bTL], dma=pfx + "tl")
        for s in range(2):
            tk.emit("dve", lambda q, s=s: q.tensor_scalar(
                TM[:, 0:width], TL[:, 0:width], CM[:, s:s + 1], None,
                op0=AluOpType.mult), reads=[bTL, bCM], writes=[bTM])
            nrow = (128 * width) // 1024
            dst = rsin[s * HALF + rows0:s * HALF + rows0 + nrow, :]
            dstv = dst.rearrange("(p two) c -> p (two c)", p=128) \
                if width == 2048 else dst.rearrange("(p one) c -> p (one c)", p=128)
            tk.emit("sp", lambda q, dstv=dstv, width=width: q.dma_start(
                out=dstv, in_=TM[:, 0:width]),
                reads=[bTM], writes=[b_rsin], dma=pfx + "rs")

    for u in range(KU):
        unit(kT[u], u * 128, kT.shape[2])
    krows = KU * (kT.shape[2] * 128 // 1024)
    vw = nk * 128
    for u in range(NT):
        unit(v[u], krows + u * (128 * vw // 1024), vw)

    b_rsout = io.setdefault("_b_rsout", Buf("rsout"))
    tk.emit("pl", lambda q: q.collective_compute(
        "ReduceScatter", AluOpType.add, replica_groups=RGROUPS,
        ins=[rsin.opt()], outs=[rsout.opt()]),
        reads=[b_rsin], writes=[b_rsout])
    io["b_rsout"] = b_rsout


def build_fused():
    nc = bass.Bass("TRN2", target_bir_lowering=False)
    E = lambda n, s, d: nc.dram_tensor(n, s, d, kind="ExternalInput").ap()
    I = lambda n, s, d: nc.dram_tensor(n, s, d).ap()
    xs = E("xs", [T, D], F32)
    out_o = nc.dram_tensor("out", [T, D], F32, kind="ExternalOutput").ap()
    ins = dict(
        l0Wq=E("l0Wq", [D, 2048], F32), l0Wk=E("l0Wk", [D, 1024], F32),
        l0Wv=E("l0Wv", [D, 1024], F32), cosT=E("cosT", [T, 64], F32),
        sinT=E("sinT", [T, 64], F32), qns=E("qns", [1, 128], F32),
        kns=E("kns", [1, 128], F32), l0Wo=E("l0Wo", [2048, D], F32),
        msc=E("msc", [128, 2], F32), cmask=E("cmask", [128, 2], F32),
        l0Wg=E("l0Wg", [D, 8192], F32), l0Wu=E("l0Wu", [D, 8192], F32),
        l0Wd=E("l0Wd", [8192, D], F32),
        l1Wq=E("l1Wq", [16, D, 128], F32), l1Wk=E("l1Wk", [16, D, 128], F32),
        l1Wv=E("l1Wv", [16, D, 128], F32), l1Wo=E("l1Wo", [2048, D], F32),
        W13=E("W13", [64, D, 256], F32), W2T=E("W2T", [8192, D], F32),
        H128=E("H128", [128, 128], F32),
    )
    qT0 = I("qT0", [NT, 128, 2048], BF16)
    kT0 = I("kT0", [NT, 128, 1024], BF16)
    v0 = I("v0", [NT, 128, 1024], BF16)
    rsin0 = I("rsin0", [2 * 2048, 1024], BF16)
    rsout0 = I("rsout0", [2048, 1024], BF16)
    x1_0 = I("x1_0", [T, D], F32)
    xlnT_0 = I("xlnT_0", [NT, 128, 2048], BF16)
    part0 = I("part0", [T, D], F32)
    x2_0 = I("x2_0", [T, D], F32)
    qT1 = I("qT1", [16, 128, T], BF16)
    kT1 = I("kT1", [16, 128, T], BF16)
    v1 = I("v1", [NT, 128, 2048], BF16)
    rsin1 = I("rsin1", [2 * 4096, 1024], BF16)
    rsout1 = I("rsout1", [4096, 1024], BF16)
    x1_1 = I("x1_1", [T, D], F32)
    xlnT_1 = I("xlnT_1", [NT, 128, 2048], BF16)

    with ExitStack() as gctx:
        tk = TK(nc, gctx)
        with ExitStack() as c:
            emit_A(nc, tk, c, 0, dict(
                xs=xs, Wq=ins["l0Wq"], Wk=ins["l0Wk"], Wv=ins["l0Wv"],
                cosT=ins["cosT"], sinT=ins["sinT"], qns=ins["qns"],
                kns=ins["kns"], qT=qT0, kT=kT0, v=v0), pfx="a0")
        barrier(tk)
        ioc0 = dict(kT=kT0, v=v0, rsin=rsin0, rsout=rsout0, cmask=ins["cmask"])
        with ExitStack() as c:
            emit_contrib(nc, tk, c, 0, ioc0, pfx="x0")
        barrier(tk)
        kTp0 = rsout0[0:1024].rearrange("(r p) c -> r p c", p=128)
        vp0 = rsout0[1024:2048].rearrange("(r p) c -> r p c", p=128)
        with ExitStack() as c:
            emit_B(nc, tk, c, 0, dict(
                qT=qT0, kTo=kT0, kTp=kTp0, vo=v0, vp=vp0, xs=xs,
                WoT=ins["l0Wo"], msc=ins["msc"], x1=x1_0, xlnT=xlnT_0),
                pfx="b0")
        barrier(tk)
        with ExitStack() as c:
            emit_C0(nc, tk, c, 0, dict(
                xlnT=xlnT_0, Wg=ins["l0Wg"][:, 0:4096], Wu=ins["l0Wu"][:, 0:4096],
                Wd=ins["l0Wd"][0:4096], out=part0), pfx="c0")
        barrier(tk)
        with ExitStack() as c:
            emit_C0(nc, tk, c, 1, dict(
                xlnT=xlnT_0, Wg=ins["l0Wg"][:, 4096:8192],
                Wu=ins["l0Wu"][:, 4096:8192], Wd=ins["l0Wd"][4096:8192],
                part=part0, x1=x1_0, out=x2_0), pfx="c1")
        barrier(tk)
        with ExitStack() as c:
            emit_A(nc, tk, c, 1, dict(
                xs=x2_0, Wq=ins["l1Wq"], Wk=ins["l1Wk"], Wv=ins["l1Wv"],
                qT=qT1, kT=kT1, v=v1), pfx="a1")
        barrier(tk)
        ioc1 = dict(kT=kT1, v=v1, rsin=rsin1, rsout=rsout1, cmask=ins["cmask"])
        with ExitStack() as c:
            emit_contrib(nc, tk, c, 1, ioc1, pfx="x1")
        barrier(tk)
        kTp1 = rsout1[0:2048].rearrange("(r p) c -> r p c", p=128)
        vp1 = rsout1[2048:4096].rearrange("(r p two) c -> r p (two c)", p=128, two=2)
        with ExitStack() as c:
            emit_B(nc, tk, c, 1, dict(
                qT=qT1, kTo=kT1, kTp=kTp1, vo=v1, vp=vp1, xs=x2_0,
                WoT=ins["l1Wo"], msc=ins["msc"], x1=x1_1, xlnT=xlnT_1),
                pfx="b1")
        barrier(tk)
        with ExitStack() as c:
            emit_C1(nc, tk, c, dict(
                xlnT=xlnT_1, W13=ins["W13"], W2T=ins["W2T"], H128=ins["H128"],
                x1=x1_1, out=out_o), pfx="d1")
        tk.finish()
    return nc


def host_prep_fused(inputs, xs_by_core):
    nc = build_fused()
    l0q = np.ascontiguousarray(
        inputs["L0_attn_q_kernel"].transpose(1, 0, 2).reshape(D, 2048))
    l0k = np.ascontiguousarray(
        inputs["L0_attn_kv_kernel"][0].transpose(1, 0, 2).reshape(D, 1024))
    l0v = np.ascontiguousarray(
        inputs["L0_attn_kv_kernel"][1].transpose(1, 0, 2).reshape(D, 1024))
    l0Wo = np.ascontiguousarray(inputs["L0_attn_out_kernel"].reshape(2048, D))
    l0Wd = np.concatenate([
        plane_perm_rows(inputs["L0_ffn_w_down"][0:4096]),
        plane_perm_rows(inputs["L0_ffn_w_down"][4096:8192])], axis=0)
    l1q = np.ascontiguousarray(inputs["L1_attn_W_Q"].transpose(0, 2, 1))
    l1k = np.ascontiguousarray(inputs["L1_attn_W_K"].transpose(0, 2, 1))
    l1v = np.ascontiguousarray(inputs["L1_attn_W_V"].transpose(0, 2, 1))
    l1Wo = np.ascontiguousarray(inputs["L1_attn_W_O"].T)
    W1T = inputs["L1_ffn_W1"].T
    W3T = inputs["L1_ffn_W3"].T
    W13 = np.ascontiguousarray(np.concatenate([
        W1T.reshape(D, 64, 128).transpose(1, 0, 2),
        W3T.reshape(D, 64, 128).transpose(1, 0, 2)], axis=2))
    W2T = np.ascontiguousarray(inputs["L1_ffn_W2"].T)
    H128 = _hadamard128()
    qns = np.ascontiguousarray(inputs["L0_attn_q_norm_scale"].reshape(1, 128))
    kns = np.ascontiguousarray(inputs["L0_attn_k_norm_scale"].reshape(1, 128))
    in_maps = []
    for c in range(8):
        h = c % 2
        pos = (np.arange(T) // 128 * 2 + h) * 128 + (np.arange(T) % 128)
        j = np.arange(64)
        ts = 10000.0 ** (2 * j / 128.0)
        ang = pos[:, None] / ts[None, :]
        msc = np.zeros((128, 2), np.float32)
        msc[:, 0] = NEG if h == 0 else 0.0
        msc[:, 1] = 0.0 if h == 0 else NEG
        cmask = np.zeros((128, 2), np.float32)
        cmask[:, 0] = 1.0 if h == 1 else 0.0
        cmask[:, 1] = 1.0 if h == 0 else 0.0
        in_maps.append({
            "xs": xs_by_core[c], "l0Wq": l0q, "l0Wk": l0k, "l0Wv": l0v,
            "cosT": np.cos(ang).astype(np.float32),
            "sinT": np.sin(ang).astype(np.float32),
            "qns": qns, "kns": kns, "l0Wo": l0Wo, "msc": msc, "cmask": cmask,
            "l0Wg": inputs["L0_ffn_w_gate"], "l0Wu": inputs["L0_ffn_w_up"],
            "l0Wd": l0Wd, "l1Wq": l1q, "l1Wk": l1k, "l1Wv": l1v,
            "l1Wo": l1Wo, "W13": W13, "W2T": W2T, "H128": H128})
    return nc, in_maps


def kernel_fused(**inputs):
    x = np.ascontiguousarray(inputs["L0_attn_x"], dtype=np.float32)
    xs = shard_x(x)
    nc, in_maps = host_prep_fused(inputs, xs)
    res = run_bass_kernel_spmd(nc, in_maps, core_ids=list(range(8)))
    return unshard_x([res.results[c]["out"] for c in range(8)])


# version 05 (final)
# speedup vs baseline: 1.1317x; 1.1317x over previous
from contextlib import ExitStack
import math
import numpy as np
import ml_dtypes

import concourse.bass as bass
import concourse.mybir as mybir
from concourse.alu_op_type import AluOpType
from concourse.bass_utils import run_bass_kernel_spmd

F32 = mybir.dt.float32
BF16 = mybir.dt.bfloat16
AF = mybir.ActivationFunctionType
AX = mybir.AxisListType

D = 2048
NT = 8
T = NT * 128
NEG = -1.0e30
EPS_LN = 1e-5
EPS_RMS = 1e-6
ATTN_SCALE = 0.08838834764831845

ENGS = ("sp", "act", "pe", "dve", "pl")
_SECT = {"sp": "sync", "act": "scalar", "pe": "tensor", "dve": "vector", "pl": "gpsimd"}


class Buf:
    __slots__ = ("w", "r", "name")

    def __init__(self, name=""):
        self.w = None
        self.r = []
        self.name = name


class TK:

    def __init__(self, nc, ctx):
        self.nc = nc
        self.ctx = ctx
        self.q = {"sp": nc.sync, "act": nc.scalar, "pe": nc.tensor,
                  "dve": nc.vector, "pl": nc.gpsimd}
        self.sem = {s: ctx.enter_context(nc.semaphore(f"S{s}")) for s in ENGS}
        self.cnt = {s: 0 for s in ENGS}
        self.hw = {e: {} for e in ENGS}
        self.ops = {e: [] for e in ENGS}

    def _sem(self, name):
        if name not in self.sem:
            self.sem[name] = self.ctx.enter_context(
                self.nc.semaphore(f"S{name}"))
            self.cnt[name] = 0
        return self.sem[name]

    def emit(self, eng, fn, reads=(), writes=(), dma=None):
        deps = {}

        def dep(tok):
            if tok is not None:
                s2, c2 = tok
                if c2 > deps.get(s2, 0):
                    deps[s2] = c2

        for b in reads:
            dep(b.w)
        for b in writes:
            dep(b.w)
            for tok in b.r:
                dep(tok)
        waits = []
        hw = self.hw[eng]
        for s2, c2 in deps.items():
            if hw.get(s2, 0) < c2:
                waits.append((s2, c2))
                hw[s2] = c2
        if dma is None:
            sname, inc = eng, 1
        else:
            sname, inc = "D_" + dma, 16
            self._sem(sname)
        self.cnt[sname] += inc
        tok = (sname, self.cnt[sname])
        self.ops[eng].append((waits, fn, inc, sname))
        for b in reads:
            b.r.append(tok)
        for b in writes:
            b.w = tok
            b.r = []
        return tok

    def wait_all_on(self, eng):
        waits = []
        hw = self.hw[eng]
        for s2, c2 in self.cnt.items():
            if s2 != eng and c2 > hw.get(s2, 0):
                waits.append((s2, c2))
                hw[s2] = c2
        self.cnt[eng] += 1
        self.ops[eng].append((waits, lambda q: q.nop(), 1, eng))

    def finish(self):
        waits = []
        hw = self.hw["sp"]
        for s2, c2 in self.cnt.items():
            if s2 != "sp" and c2 > hw.get(s2, 0):
                waits.append((s2, c2))
                hw[s2] = c2
        self.cnt["sp"] += 1
        self.ops["sp"].append((waits, lambda q: q.nop(), 1, "sp"))

        with self.nc.Block() as block:
            for e in ENGS:
                sect = getattr(block, _SECT[e])
                ops = self.ops[e]
                sem = self.sem

                def section(q, ops=ops):
                    for (waits, fn, inc, sname) in ops:
                        for (s2, c2) in waits:
                            q.wait_ge(sem[s2], c2)
                        inst = fn(q)
                        if inst is not None:
                            inst.then_inc(sem[sname], inc)

                sect(section)




class Mem:
    _seq = 0

    def __init__(self, nc, ctx):
        self.nc = nc
        self.ctx = ctx
        self.bpp = 0
        self.psum_banks = 0
        Mem._seq += 1
        self.tag = f"_{Mem._seq}"

    def sb(self, name, shape, dtype):
        t = self.ctx.enter_context(
            self.nc.sbuf_tensor(name + self.tag, list(shape), dtype))
        esz = 2 if dtype == BF16 else 4
        n = 1
        for s in shape[1:]:
            n *= s
        self.bpp += n * esz
        return t[:], Buf(name)

    def ps(self, name, shape, dtype):
        t = self.ctx.enter_context(
            self.nc.psum_tensor(name + self.tag, list(shape), dtype))
        esz = 2 if dtype == BF16 else 4
        n = 1
        for s in shape[1:]:
            n *= s
        self.psum_banks += (n * esz + 2047) // 2048
        return t[:], Buf(name)


def emit_identity(tk, mem):
    ident, b = mem.sb("identB", [128, 128], BF16)
    tk.emit("pl", lambda q: q.memset(ident, 0.0), writes=[b])
    tk.emit("pl", lambda q: q.affine_select(
        out=ident, in_=ident, compare_op=AluOpType.not_equal, fill=1.0,
        base=0, pattern=[[-1, 128]], channel_multiplier=1),
        reads=[b], writes=[b])
    return ident, b


def emit_ln(tk, st, X, bX, xln, bxln, j1, j2):
    tk.emit("dve", lambda q: q.tensor_scalar(
        j1[0], X, 1.0 / D, None, op0=AluOpType.mult, op1=AluOpType.add,
        accum_out=st["mean"][0]), reads=[bX], writes=[st["mean"][1], j1[1]])
    tk.emit("act", lambda q: q.activation(
        j2[0], X, AF.Square, accum_out=st["ssq"][0]),
        reads=[bX], writes=[st["ssq"][1], j2[1]])
    tk.emit("dve", lambda q: q.tensor_tensor(
        st["var"][0], st["mean"][0], st["mean"][0], op=AluOpType.mult),
        reads=[st["mean"][1]], writes=[st["var"][1]])
    tk.emit("dve", lambda q: q.scalar_tensor_tensor(
        st["var"][0], st["ssq"][0], 1.0 / D, st["var"][0],
        op0=AluOpType.mult, op1=AluOpType.subtract),
        reads=[st["ssq"][1], st["var"][1]], writes=[st["var"][1]])
    tk.emit("act", lambda q: q.activation(
        st["rstd"][0], st["var"][0], AF.Ln, bias=st["epln"][0]),
        reads=[st["var"][1], st["epln"][1]], writes=[st["rstd"][1]])
    tk.emit("act", lambda q: q.activation(
        st["rstd"][0], st["rstd"][0], AF.Exp, scale=-0.5),
        reads=[st["rstd"][1]], writes=[st["rstd"][1]])
    tk.emit("dve", lambda q: q.scalar_tensor_tensor(
        st["negmb"][0], st["mean"][0], -1.0, st["rstd"][0],
        op0=AluOpType.mult, op1=AluOpType.mult),
        reads=[st["mean"][1], st["rstd"][1]], writes=[st["negmb"][1]])
    tk.emit("act", lambda q: q.activation(
        xln, X, AF.Identity, bias=st["negmb"][0], scale=st["rstd"][0]),
        reads=[bX, st["rstd"][1], st["negmb"][1]], writes=[bxln])


def mk_stats(mem):
    st = {n: mem.sb("st_" + n, [128, 1], F32) for n in
          ("mean", "ssq", "var", "rstd", "negmb")}
    return st


def const_col(tk, mem, name, value):
    ap, b = mem.sb(name, [128, 1], F32)
    tk.emit("pl", lambda q: q.memset(ap, value), writes=[b])
    return ap, b


def emit_tr16(tk, ident, b_id, src, bsrc, dst, bdst, nblk, psT, width=128):
    done = 0
    bank = 0
    while done < nblk:
        k = min(8, nblk - done)
        pb, bpb = psT[bank % len(psT)]

        def tg(q, src=src, pb=pb, done=done, k=k):
            last = None
            for i in range(k):
                last = q.matmul(pb[:, i * 128:(i + 1) * 128],
                                src[:, (done + i) * 128:(done + i + 1) * 128],
                                ident, start=True, stop=True,
                                is_transpose=True, skip_group_check=True)
            return last
        tk.emit("pe", tg, reads=[bsrc, b_id], writes=[bpb])
        tk.emit("dve", lambda q, pb=pb, done=done, k=k: q.tensor_copy(
            dst[:, done * 128:(done + k) * 128], pb[:, 0:k * 128]),
            reads=[bpb], writes=[bdst])
        done += k
        bank += 1



def emit_A(nc, tk, ctx, layer, io, pfx=""):
    nk = 8 if layer == 0 else 16
    xs = io["xs"]; Wq = io["Wq"]; Wk = io["Wk"]; Wv = io["Wv"]
    qT_o = io["qT"]; kT_o = io["kT"]; v_o = io["v"]
    if layer == 0:
        cosT = io["cosT"]; sinT = io["sinT"]
        qns = io["qns"]; kns = io["kns"]
    mem = Mem(nc, ctx)
    ident, b_id = emit_identity(tk, mem)
    st = mk_stats(mem)
    st["epln"] = const_col(tk, mem, "epln", EPS_LN)
    eprms = const_col(tk, mem, "eprms", EPS_RMS)
    lsc = const_col(tk, mem, "lsc", math.log(ATTN_SCALE))

    X = [mem.sb(f"X{i}", [128, D], F32) for i in range(2)]
    XLN, bXLN = mem.sb("XLN", [128, D], BF16)
    XLT = [mem.sb(f"XLT{i}", [128, D], BF16) for i in range(2)]
    psT = [mem.ps(f"psT{i}", [128, 1024], BF16) for i in range(2)]
    psM = [mem.ps(f"psM{i}", [128, 512], F32) for i in range(4)]

    if layer == 0:
        WqS, bWq = mem.sb("WqS", [128, 16 * 2048], BF16)
        WkS, bWk = mem.sb("WkS", [128, 16 * nk * 128], BF16)
        WvS, bWv = mem.sb("WvS", [128, 16 * nk * 128], BF16)
        tk.emit("pl", lambda q: q.dma_start(
            out=WqS.rearrange("p (c n) -> p c n", c=16),
            in_=Wq), writes=[bWq], dma=pfx + "wq")
        tk.emit("pl", lambda q: q.dma_start(
            out=WkS.rearrange("p (c n) -> p c n", c=16),
            in_=Wk), writes=[bWk], dma=pfx + "wk")
        tk.emit("pl", lambda q: q.dma_start(
            out=WvS.rearrange("p (c n) -> p c n", c=16),
            in_=Wv), writes=[bWv], dma=pfx + "wv")

        COS, bCOS = mem.sb("COS", [128, NT * 64], F32)
        SIN, bSIN = mem.sb("SIN", [128, NT * 64], F32)
        tk.emit("sp", lambda q: q.dma_start(
            out=COS.rearrange("p (r j) -> p r j", r=NT),
            in_=cosT.rearrange("(r p) j -> p r j", p=128)), writes=[bCOS], dma=pfx + "cos")
        tk.emit("sp", lambda q: q.dma_start(
            out=SIN.rearrange("p (r j) -> p r j", r=NT),
            in_=sinT.rearrange("(r p) j -> p r j", p=128)), writes=[bSIN], dma=pfx + "sin")
        NSQ, bNSQ = mem.sb("NSQ", [1, 128], BF16)
        NSK, bNSK = mem.sb("NSK", [1, 128], BF16)
        tk.emit("pl", lambda q: q.dma_start(out=NSQ, in_=qns), writes=[bNSQ], dma=pfx + "nsq")
        tk.emit("pl", lambda q: q.dma_start(out=NSK, in_=kns), writes=[bNSK], dma=pfx + "nsk")
        ONE1, bONE = mem.sb("ONE1", [1, 128], BF16)
        tk.emit("pl", lambda q: q.memset(ONE1, 1.0), writes=[bONE])
        QSB, bQSB = mem.sb("QSB", [128, 128], F32)
        KSB, bKSB = mem.sb("KSB", [128, 128], F32)
        for (SB, bSB, NS, bNS) in ((QSB, bQSB, NSQ, bNSQ), (KSB, bKSB, NSK, bNSK)):
            pb, bpb = psM[0]
            tk.emit("pe", lambda q, NS=NS, pb=pb: q.matmul(
                pb[:, 0:128], ONE1, NS, start=True, stop=True),
                reads=[bONE, bNS], writes=[bpb])
            tk.emit("dve", lambda q, SB=SB, pb=pb: q.tensor_copy(SB, pb[:, 0:128]),
                    reads=[bpb], writes=[bSB])

        Q, bQ = mem.sb("Q", [128, 2048], F32)
        K, bK = mem.sb("K", [128, nk * 128], F32)
        QSQ, bQSQ = mem.sb("QSQ", [128, 2048], F32)
        SSH, bSSH = mem.sb("SSH", [128, 16], F32)
        RSD, bRSD = mem.sb("RSD", [128, 16], F32)
        SSHK, bSSHK = mem.sb("SSHK", [128, 8], F32)
        RSDK, bRSDK = mem.sb("RSDK", [128, 8], F32)
        RT1, bRT1 = mem.sb("RT1", [128, 1024], F32)
        RT2, bRT2 = mem.sb("RT2", [128, 1024], F32)
        QR, bQR = mem.sb("QR", [128, 2048], BF16)
        KR, bKR = mem.sb("KR", [128, nk * 128], BF16)
        VT, bVT = mem.sb("VT", [128, nk * 128], BF16)
        STG, bSTG = XLN, bXLN
        KSTG, bKSTG = mem.sb("KSTG", [128, nk * 128], BF16)

        for r in range(NT):
            Xa, bXa = X[r % 2]
            tk.emit("sp", lambda q, r=r, Xa=Xa: q.dma_start(
                out=Xa, in_=xs[r * 128:(r + 1) * 128, :]),
                writes=[bXa], dma=pfx + "x%d" % (r % 2))
            emit_ln(tk, st, Xa, bXa, XLN, bXLN, (QR, bQR), (QSQ, bQSQ))
            XLTa, bXLT = XLT[r % 2]
            emit_tr16(tk, ident, b_id, XLN, bXLN, XLTa, bXLT, 16, psT)

            def proj(W, bW, wwid, slab, pb, bpb):
                nsl = wwid // 512

                def g(q, W=W, slab=slab, pb=pb, nsl=nsl, XLTa=XLTa):
                    last = None
                    for c in range(16):
                        last = q.matmul(
                            pb, XLTa[:, c * 128:(c + 1) * 128],
                            W[:, (c * nsl + slab) * 512:(c * nsl + slab + 1) * 512],
                            start=(c == 0), stop=(c == 15))
                    return last
                tk.emit("pe", g, reads=[bXLT, bW], writes=[bpb])

            for s in range(4):
                pb, bpb = psM[s % 2]
                proj(WqS, bWq, 2048, s, pb, bpb)
                tk.emit("dve", lambda q, pb=pb, s=s: q.tensor_copy(
                    Q[:, s * 512:(s + 1) * 512], pb), reads=[bpb], writes=[bQ])
            for s in range(2):
                pb, bpb = psM[2 + s]
                proj(WkS, bWk, 1024, s, pb, bpb)
                tk.emit("dve", lambda q, pb=pb, s=s: q.tensor_copy(
                    K[:, s * 512:(s + 1) * 512], pb), reads=[bpb], writes=[bK])
            for s in range(2):
                pb, bpb = psM[s]
                proj(WvS, bWv, 1024, s, pb, bpb)
                tk.emit("dve", lambda q, pb=pb, s=s: q.tensor_copy(
                    VT[:, s * 512:(s + 1) * 512], pb), reads=[bpb], writes=[bVT])

            tk.emit("act", lambda q: q.activation(QSQ, Q, AF.Square),
                    reads=[bQ], writes=[bQSQ])
            tk.emit("dve", lambda q: q.tensor_reduce(
                SSH, QSQ.rearrange("p (n h) -> p n h", n=16), axis=AX.X,
                op=AluOpType.add), reads=[bQSQ], writes=[bSSH])
            tk.emit("act", lambda q: q.activation(
                RSD, SSH, AF.Ln, bias=eprms[0], scale=1.0 / 128),
                reads=[bSSH, eprms[1]], writes=[bRSD])
            tk.emit("act", lambda q: q.activation(
                RSD, RSD, AF.Exp, scale=-0.5, bias=lsc[0]),
                reads=[bRSD, lsc[1]], writes=[bRSD])
            tk.emit("act", lambda q: q.activation(
                QSQ[:, 0:1024], K, AF.Square), reads=[bK], writes=[bQSQ])
            tk.emit("dve", lambda q: q.tensor_reduce(
                SSHK, QSQ[:, 0:1024].rearrange("p (n h) -> p n h", n=8),
                axis=AX.X, op=AluOpType.add), reads=[bQSQ], writes=[bSSHK])
            tk.emit("act", lambda q: q.activation(
                RSDK, SSHK, AF.Ln, bias=eprms[0], scale=1.0 / 128),
                reads=[bSSHK, eprms[1]], writes=[bRSDK])
            tk.emit("act", lambda q: q.activation(
                RSDK, RSDK, AF.Exp, scale=-0.5), reads=[bRSDK], writes=[bRSDK])
            tk.emit("dve", lambda q: q.tensor_tensor(
                QSQ.rearrange("p (n h) -> p n h", n=16),
                Q.rearrange("p (n h) -> p n h", n=16),
                QSB.unsqueeze(1).broadcast_to([128, 16, 128]),
                op=AluOpType.mult), reads=[bQ, bQSB], writes=[bQSQ])

            def rope(S, bS, nh, RS, bRS, dst, bdst):
                s4 = S[:, 0:nh * 128].rearrange("p (n t j) -> p n t j", n=nh, t=2)
                q1v = s4[:, :, 0:1, :].squeeze(2)
                q2v = s4[:, :, 1:2, :].squeeze(2)
                cb = COS[:, r * 64:(r + 1) * 64].unsqueeze(1).broadcast_to(
                    [128, nh, 64])
                sb2 = SIN[:, r * 64:(r + 1) * 64].unsqueeze(1).broadcast_to(
                    [128, nh, 64])
                rv = RT1[:, 0:nh * 64].rearrange("p (n j) -> p n j", n=nh)
                rv2 = RT2[:, 0:nh * 64].rearrange("p (n j) -> p n j", n=nh)
                d4 = dst[:, 0:nh * 128].rearrange("p (n t j) -> p n t j", n=nh, t=2)
                d1v = d4[:, :, 0:1, :].squeeze(2)
                d2v = d4[:, :, 1:2, :].squeeze(2)
                rsb = RS[:, 0:nh].unsqueeze(2).broadcast_to([128, nh, 64])
                tk.emit("dve", lambda q: q.tensor_tensor(rv, q1v, cb,
                        op=AluOpType.mult), reads=[bS, bCOS], writes=[bRT1])
                tk.emit("dve", lambda q: q.tensor_tensor(rv2, q2v, sb2,
                        op=AluOpType.mult), reads=[bS, bSIN], writes=[bRT2])
                tk.emit("dve", lambda q: q.tensor_tensor(rv, rv, rv2,
                        op=AluOpType.subtract), reads=[bRT1, bRT2], writes=[bRT1])
                tk.emit("dve", lambda q: q.tensor_tensor(d1v, rv, rsb,
                        op=AluOpType.mult), reads=[bRT1, bRS], writes=[bdst])
                tk.emit("dve", lambda q: q.tensor_tensor(rv, q2v, cb,
                        op=AluOpType.mult), reads=[bS, bCOS], writes=[bRT1])
                tk.emit("dve", lambda q: q.tensor_tensor(rv2, q1v, sb2,
                        op=AluOpType.mult), reads=[bS, bSIN], writes=[bRT2])
                tk.emit("dve", lambda q: q.tensor_tensor(rv, rv, rv2,
                        op=AluOpType.add), reads=[bRT1, bRT2], writes=[bRT1])
                tk.emit("dve", lambda q: q.tensor_tensor(d2v, rv, rsb,
                        op=AluOpType.mult), reads=[bRT1, bRS], writes=[bdst])

            rope(QSQ, bQSQ, 16, RSD, bRSD, QR, bQR)
            tk.emit("dve", lambda q: q.tensor_tensor(
                QSQ[:, 0:1024].rearrange("p (n h) -> p n h", n=8),
                K.rearrange("p (n h) -> p n h", n=8),
                KSB.unsqueeze(1).broadcast_to([128, 8, 128]),
                op=AluOpType.mult), reads=[bK, bKSB], writes=[bQSQ])
            rope(QSQ, bQSQ, 8, RSDK, bRSDK, KR, bKR)

            emit_tr16(tk, ident, b_id, QR, bQR, STG, bSTG, 16, psT)
            tk.emit("sp", lambda q, r=r: q.dma_start(out=qT_o[r], in_=STG),
                    reads=[bSTG], dma=pfx + "qst")
            emit_tr16(tk, ident, b_id, KR, bKR, KSTG, bKSTG, 8, psT)
            tk.emit("sp", lambda q, r=r: q.dma_start(out=kT_o[r], in_=KSTG),
                    reads=[bKSTG], dma=pfx + "kst")
            tk.emit("sp", lambda q, r=r: q.dma_start(out=v_o[r], in_=VT),
                    reads=[bVT], dma=pfx + "vst")
    else:
        XLTF, bXLTF = mem.sb("XLTF", [128, NT * 2048], BF16)
        J1, bJ1 = mem.sb("J1", [128, D], BF16)
        J2, bJ2 = mem.sb("J2", [128, D], F32)
        for r in range(NT):
            Xa, bXa = X[r % 2]
            tk.emit("sp", lambda q, r=r, Xa=Xa: q.dma_start(
                out=Xa, in_=xs[r * 128:(r + 1) * 128, :]),
                writes=[bXa], dma=pfx + "x%d" % (r % 2))
            emit_ln(tk, st, Xa, bXa, XLN, bXLN, (J1, bJ1), (J2, bJ2))
            emit_tr16(tk, ident, b_id, XLN, bXLN,
                      XLTF[:, r * 2048:(r + 1) * 2048], bXLTF, 16, psT)

        WH = [mem.sb(f"WH{i}", [128, 16 * 128], BF16) for i in range(2)]
        HSTG, bHSTG = mem.sb("HSTG", [128, 1024], BF16)
        VSTG = [mem.sb(f"VSTG{i}", [128, 2048], BF16) for i in range(NT)]
        xv = XLTF.rearrange("p (r c t) -> p c r t", c=16, t=128)

        def headmm(Wsrc, n, bWH, WHa, wkey, out_dst, bdst, scale, to_v=False):
            tk.emit("pl", lambda q, Wsrc=Wsrc, n=n, WHa=WHa: q.dma_start(
                out=WHa.rearrange("p (c h) -> p c h", c=16),
                in_=Wsrc[n]),
                writes=[bWH], dma=wkey)
            for m in range(2):
                pb, bpb = psM[m]

                def g(q, WHa=WHa, m=m, pb=pb):
                    last = None
                    for c in range(16):
                        last = q.matmul(
                            pb, WHa[:, c * 128:(c + 1) * 128],
                            xv[:, c, m * 4:(m + 1) * 4, :],
                            start=(c == 0), stop=(c == 15))
                    return last
                tk.emit("pe", g, reads=[bXLTF, bWH], writes=[bpb])
                tk.emit("act", lambda q, pb=pb, m=m, scale=scale: q.activation(
                    HSTG[:, m * 512:(m + 1) * 512], pb, AF.Copy, scale=scale),
                    reads=[bpb], writes=[bHSTG])
            if to_v:
                for r in range(NT):
                    pb, bpb = psT[r % 2]
                    tk.emit("pe", lambda q, r=r, pb=pb: q.matmul(
                        pb[:, 0:128], HSTG[:, r * 128:(r + 1) * 128], ident,
                        start=True, stop=True, is_transpose=True,
                        skip_group_check=True),
                        reads=[bHSTG, b_id], writes=[bpb])
                    Va, bVa = VSTG[r]
                    tk.emit("dve", lambda q, pb=pb, Va=Va, n=n: q.tensor_copy(
                        Va[:, n * 128:(n + 1) * 128], pb[:, 0:128]),
                        reads=[bpb], writes=[bVa])
            else:
                tk.emit("sp", lambda q, out_dst=out_dst, n=n: q.dma_start(
                    out=out_dst[n], in_=HSTG), reads=[bHSTG], dma=pfx + "hst")

        for n in range(16):
            WHa, bWH = WH[n % 2]
            headmm(Wq, n, bWH, WHa, "wh%d" % (n % 2), qT_o, None, ATTN_SCALE)
            WHa2, bWH2 = WH[(n + 1) % 2]
            headmm(Wk, n, bWH2, WHa2, "wh%d" % ((n + 1) % 2), kT_o, None, 1.0)
            headmm(Wv, n, bWH, WHa, "wh%d" % (n % 2), None, None, 1.0, to_v=True)
        for r in range(NT):
            Va, bVa = VSTG[r]
            tk.emit("sp", lambda q, r=r, Va=Va: q.dma_start(
                out=v_o[r], in_=Va), reads=[bVa], dma=pfx + "vs%d" % r)

    assert mem.bpp <= 207000, f"SBUF over budget: {mem.bpp}"


def build_A(layer):
    nc = bass.Bass("TRN2", target_bir_lowering=False)
    nk = 8 if layer == 0 else 16
    io = {}
    io["xs"] = nc.dram_tensor("xs", [T, D], F32, kind="ExternalInput").ap()
    if layer == 0:
        io["Wq"] = nc.dram_tensor("Wq", [128, 16, 2048], F32, kind="ExternalInput").ap()
        io["Wk"] = nc.dram_tensor("Wk", [128, 16, nk * 128], F32, kind="ExternalInput").ap()
        io["Wv"] = nc.dram_tensor("Wv", [128, 16, nk * 128], F32, kind="ExternalInput").ap()
        io["cosT"] = nc.dram_tensor("cosT", [T, 64], F32, kind="ExternalInput").ap()
        io["sinT"] = nc.dram_tensor("sinT", [T, 64], F32, kind="ExternalInput").ap()
        io["qns"] = nc.dram_tensor("qns", [1, 128], F32, kind="ExternalInput").ap()
        io["kns"] = nc.dram_tensor("kns", [1, 128], F32, kind="ExternalInput").ap()
        io["qT"] = nc.dram_tensor("qT", [NT, 128, 2048], BF16, kind="ExternalOutput").ap()
        io["kT"] = nc.dram_tensor("kT", [NT, 128, nk * 128], BF16, kind="ExternalOutput").ap()
    else:
        io["Wq"] = nc.dram_tensor("Wq", [16, 128, 16, 128], F32, kind="ExternalInput").ap()
        io["Wk"] = nc.dram_tensor("Wk", [nk, 128, 16, 128], F32, kind="ExternalInput").ap()
        io["Wv"] = nc.dram_tensor("Wv", [nk, 128, 16, 128], F32, kind="ExternalInput").ap()
        io["qT"] = nc.dram_tensor("qT", [16, 128, T], BF16, kind="ExternalOutput").ap()
        io["kT"] = nc.dram_tensor("kT", [nk, 128, T], BF16, kind="ExternalOutput").ap()
    io["v"] = nc.dram_tensor("v", [NT, 128, nk * 128], BF16, kind="ExternalOutput").ap()
    with ExitStack() as gctx:
        tk = TK(nc, gctx)
        with ExitStack() as ctx:
            emit_A(nc, tk, ctx, layer, io)
        tk.finish()
    return nc




def host_prep_A(layer, inputs, xs_by_core):
    nc = build_A(layer)
    in_maps = []
    if layer == 0:
        qk = _pmaj16(inputs["L0_attn_q_kernel"].transpose(1, 0, 2).reshape(D, 2048))
        kk = _pmaj16(inputs["L0_attn_kv_kernel"][0].transpose(1, 0, 2).reshape(D, 1024))
        vk = _pmaj16(inputs["L0_attn_kv_kernel"][1].transpose(1, 0, 2).reshape(D, 1024))
        qns = inputs["L0_attn_q_norm_scale"].reshape(1, 128)
        kns = inputs["L0_attn_k_norm_scale"].reshape(1, 128)
        for c in range(8):
            h = c % 2
            pos = (np.arange(T) // 128 * 2 + h) * 128 + (np.arange(T) % 128)
            j = np.arange(64)
            ts = 10000.0 ** (2 * j / 128.0)
            ang = pos[:, None] / ts[None, :]
            in_maps.append({
                "xs": xs_by_core[c], "Wq": qk, "Wk": kk, "Wv": vk,
                "cosT": np.cos(ang).astype(np.float32),
                "sinT": np.sin(ang).astype(np.float32),
                "qns": qns, "kns": kns})
    else:
        wq = _pmaj16h(inputs["L1_attn_W_Q"].transpose(0, 2, 1))
        wk = _pmaj16h(inputs["L1_attn_W_K"].transpose(0, 2, 1))
        wv = _pmaj16h(inputs["L1_attn_W_V"].transpose(0, 2, 1))
        for c in range(8):
            in_maps.append({"xs": xs_by_core[c], "Wq": wq, "Wk": wk, "Wv": wv})
    return nc, in_maps


def _pmaj16(w):
    return np.ascontiguousarray(w.reshape(16, 128, w.shape[1]).transpose(1, 0, 2))


def _pmaj16h(w):
    H, Dd, N = w.shape
    return np.ascontiguousarray(
        w.reshape(H, 16, 128, N).transpose(0, 2, 1, 3))


def _pmaj_slab(w, slab=512):
    Dd, I = w.shape
    ns = I // slab
    return np.ascontiguousarray(
        w.reshape(16, 128, ns, slab).transpose(2, 1, 0, 3))


def shard_x(x):
    xb = x.reshape(4, 16, 128, D)
    out = []
    for c in range(8):
        b, h = c // 2, c % 2
        out.append(np.ascontiguousarray(
            xb[b, h::2].reshape(T, D)))
    return out


def unshard_x(parts):
    x = np.zeros((4, 16, 128, D), dtype=parts[0].dtype)
    for c in range(8):
        b, h = c // 2, c % 2
        x[b, h::2] = parts[c].reshape(NT, 128, D)
    return x.reshape(4, 2048, D)





def wdt_of(layer, r):
    return (min(r, 4) + 1) if layer == 0 else (r + 1)


def emit_B(nc, tk, ctx, layer, io, pfx=""):
    nk = 8 if layer == 0 else 16
    qT_i = io["qT"]; kTo_i = io["kTo"]; kTp_i = io["kTp"]
    vo_i = io["vo"]; vp_i = io["vp"]; xs = io["xs"]
    WoT = io["WoT"]; msc = io["msc"]
    x1_o = io["x1"]; xlnT_o = io["xlnT"]
    mem = Mem(nc, ctx)
    ident, b_id = emit_identity(tk, mem)
    st = mk_stats(mem)
    st["epln"] = const_col(tk, mem, "epln", EPS_LN)

    QT, bQT = mem.sb("QT", [128, 16384], BF16)
    tk.emit("sp", lambda q: q.dma_start(
        out=QT.rearrange("p (r c) -> p r c", r=qT_i.shape[0]),
        in_=qT_i.rearrange("r p c -> p r c")), writes=[bQT], dma=pfx + "qt")
    if layer == 0:
        KTO, bKTO = mem.sb("KTO", [128, NT * 1024], BF16)
        KTP, bKTP = mem.sb("KTP", [128, NT * 1024], BF16)
        tk.emit("sp", lambda q: q.dma_start(
            out=KTO.rearrange("p (r c) -> p r c", r=NT),
            in_=kTo_i.rearrange("r p c -> p r c")), writes=[bKTO], dma=pfx + "kto")
        tk.emit("sp", lambda q: q.dma_start(
            out=KTP.rearrange("p (r c) -> p r c", r=NT),
            in_=kTp_i.rearrange("r p c -> p r c")), writes=[bKTP], dma=pfx + "ktp")
    else:
        KH = [mem.sb(f"KH{i}", [128, 2048], BF16) for i in range(2)]
    VO, bVO = mem.sb("VO", [128, NT * nk * 128], BF16)
    VP, bVP = mem.sb("VP", [128, NT * nk * 128], BF16)
    tk.emit("sp", lambda q: q.dma_start(
        out=VO.rearrange("p (r c) -> p r c", r=NT),
        in_=vo_i.rearrange("r p c -> p r c")), writes=[bVO], dma=pfx + "vo")
    tk.emit("sp", lambda q: q.dma_start(
        out=VP.rearrange("p (r c) -> p r c", r=NT),
        in_=vp_i.rearrange("r p c -> p r c")), writes=[bVP], dma=pfx + "vp")

    MS, bMS = mem.sb("MS", [128, 2], F32)
    tk.emit("sp", lambda q: q.dma_start(out=MS, in_=msc), writes=[bMS], dma=pfx + "ms")
    TRI, bTRI = mem.sb("TRI", [128, 128], F32)
    tk.emit("pl", lambda q: q.memset(TRI, 0.0), writes=[bTRI])
    tk.emit("pl", lambda q: q.affine_select(
        out=TRI, in_=TRI, compare_op=AluOpType.is_ge, fill=NEG,
        base=0, pattern=[[-1, 128]], channel_multiplier=1),
        reads=[bTRI], writes=[bTRI])
    if layer == 0:
        EDG, bEDG = mem.sb("EDG", [128, 128], F32)
        tk.emit("pl", lambda q: q.memset(EDG, 0.0), writes=[bEDG])
        tk.emit("pl", lambda q: q.affine_select(
            out=EDG, in_=EDG, compare_op=AluOpType.is_ge, fill=NEG,
            base=-1, pattern=[[1, 128]], channel_multiplier=-1),
            reads=[bEDG], writes=[bEDG])

    ET, bET = mem.sb("ET", [128, 16 * 1024], BF16)
    ENC, bENC = mem.sb("ENC", [128, 16 * 1024], BF16)
    SSp = [mem.sb(f"SS{i}", [128, 2048], F32) for i in range(2)]
    EE, bEE = mem.sb("EE", [128, 2048], BF16)
    EFp = [mem.sb(f"EF{i}", [128, 2048], BF16) for i in range(2)]
    EF, bEF = EFp[0]
    nmp = [mem.sb(f"nmv{i}", [128, 1], F32) for i in range(2)]
    lp = [mem.sb(f"lv{i}", [128, 1], F32) for i in range(2)]
    rlp = [mem.sb(f"rlv{i}", [128, 1], F32) for i in range(2)]

    S4, _ = mem.ps("S4", [128, 2048], F32)
    bS4 = [Buf(f"S4_{i}") for i in range(4)]
    psT = [mem.ps(f"psT{i}", [128, 1024], BF16) for i in range(2)]
    psE = [mem.ps(f"psE{i}", [128, 512], F32) for i in range(2)]

    def qslice(n, r):
        if layer == 0:
            return QT[:, r * 2048 + n * 128: r * 2048 + (n + 1) * 128]
        return QT[:, n * 1024 + r * 128: n * 1024 + (r + 1) * 128]

    ETv = ET.rearrange("p (s c) -> p s c", s=16)

    for n in range(16):
        kv = n // 2 if layer == 0 else n
        if layer == 1:
            KHo, bKHo = KH[0]
            KHp, bKHp = KH[1]
            tk.emit("sp", lambda q, n=n, KHo=KHo: q.dma_start(
                out=KHo[:, 0:1024], in_=kTo_i[n]), writes=[bKHo], dma=pfx + "kh0")
            tk.emit("sp", lambda q, n=n, KHp=KHp: q.dma_start(
                out=KHp[:, 0:1024], in_=kTp_i[n]), writes=[bKHp], dma=pfx + "kh1")
        def do_scores(r):
            nb = wdt_of(layer, r)
            p0 = r - nb + 1
            w = nb * 128
            SS, bSS = SSp[r % 2]
            EF, bEF = EFp[r % 2]
            nmv, bnm = nmp[r % 2]
            lv, bl = lp[r % 2]
            rlv, brl = rlp[r % 2]
            lhs = qslice(n, r)
            for (side, base_col) in ((0, 0), (1, 1024)):
                if layer == 0:
                    KT = KTO if side == 0 else KTP
                    bKT = bKTO if side == 0 else bKTP
                    ktv = KT.rearrange("p (s c) -> p s c", s=NT)

                    def g(q, ktv=ktv, p0=p0, nb=nb, kv=kv, lhs=lhs,
                          base_col=base_col):
                        last = None
                        done = 0
                        while done < nb:
                            m = min(4, nb - done)
                            rhs = ktv[:, p0 + done:p0 + done + m,
                                      kv * 128:(kv + 1) * 128]
                            last = q.matmul(
                                S4[:, base_col + done * 128:
                                   base_col + (done + m) * 128],
                                lhs, rhs, start=True, stop=True)
                            done += m
                        return last
                    rbufs = [bQT, bKT]
                else:
                    KHa, bKHa = KH[side]

                    def g(q, KHa=KHa, p0=p0, nb=nb, lhs=lhs,
                          base_col=base_col):
                        last = None
                        done = 0
                        while done < nb:
                            m = min(4, nb - done)
                            rhs = KHa[:, (p0 + done) * 128:
                                      (p0 + done + m) * 128]
                            last = q.matmul(
                                S4[:, base_col + done * 128:
                                   base_col + (done + m) * 128],
                                lhs, rhs, start=True, stop=True)
                            done += m
                        return last
                    rbufs = [bQT, bKHa]
                banks = [base_col // 512 + i for i in range((w + 511) // 512)]
                tk.emit("pe", g, reads=rbufs,
                        writes=[bS4[i] for i in banks])

            ob = [bS4[i] for i in range((w + 511) // 512)]
            pb2 = [bS4[2 + i] for i in range((w + 511) // 512)]
            edge = (layer == 0 and r >= 4)
            c0 = 128 if edge else 0
            if edge:
                tk.emit("dve", lambda q: q.tensor_tensor(
                    SS[:, 0:128], S4[:, 0:128], EDG, op=AluOpType.add),
                    reads=ob + [bEDG], writes=[bSS])
            if w - 128 > c0:
                tk.emit("act", lambda q, c0=c0, w=w: q.activation(
                    SS[:, c0:w - 128], S4[:, c0:w - 128], AF.Copy),
                    reads=ob, writes=[bSS])
            tk.emit("dve", lambda q, w=w: q.tensor_tensor(
                SS[:, w - 128:w], S4[:, w - 128:w], TRI, op=AluOpType.add),
                reads=ob + [bTRI], writes=[bSS])
            if edge:
                tk.emit("dve", lambda q: q.tensor_scalar(
                    SS[:, 1024:1152], S4[:, 1024:1152], MS[:, 1:2], None,
                    op0=AluOpType.add), reads=pb2 + [bMS], writes=[bSS])
            if w - 128 > c0:
                tk.emit("act", lambda q, c0=c0, w=w: q.activation(
                    SS[:, 1024 + c0:1024 + w - 128],
                    S4[:, 1024 + c0:1024 + w - 128], AF.Copy),
                    reads=pb2, writes=[bSS])
            tk.emit("dve", lambda q, w=w: q.tensor_scalar(
                SS[:, 1024 + w - 128:1024 + w], S4[:, 1024 + w - 128:1024 + w],
                MS[:, 0:1], None, op0=AluOpType.add),
                reads=pb2 + [bMS], writes=[bSS])

            sv = SS.rearrange("p (s c) -> p s c", s=2)[:, :, 0:w]
            ev = EE.rearrange("p (s c) -> p s c", s=2)[:, :, 0:w]
            fv = EF.rearrange("p (s c) -> p s c", s=2)[:, :, 0:w]
            tk.emit("dve", lambda q, sv=sv, nmv=nmv: q.reduce_max(
                nmv, sv, axis=AX.XY, negate=True), reads=[bSS], writes=[bnm])
            tk.emit("act", lambda q, sv=sv, ev=ev, nmv=nmv, lv=lv: q.activation(
                ev, sv, AF.Exp, bias=nmv, accum_out=lv),
                reads=[bSS, bnm], writes=[bEE, bl])
            tk.emit("dve", lambda q, rlv=rlv, lv=lv: q.reciprocal(rlv, lv),
                    reads=[bl], writes=[brl])
            tk.emit("act", lambda q, ev=ev, fv=fv, rlv=rlv: q.activation(
                fv, ev, AF.Identity, scale=rlv),
                reads=[bEE, brl], writes=[bEF])

        def do_ET(r):
            nb = wdt_of(layer, r)
            p0 = r - nb + 1
            EF, bEF = EFp[r % 2]
            for (side, base_col, slot0) in ((0, 0, p0), (1, 1024, 8 + p0)):
                pb, bpb = psT[side]

                def tg(q, pb=pb, base_col=base_col, nb=nb, EF=EF):
                    last = None
                    for i in range(nb):
                        last = q.matmul(
                            pb[:, i * 128:(i + 1) * 128],
                            EF[:, base_col + i * 128:base_col + (i + 1) * 128],
                            ident, start=True, stop=True,
                            is_transpose=True, skip_group_check=True)
                    return last
                tk.emit("pe", tg, reads=[bEF, b_id], writes=[bpb])
                dstv = ETv[:, slot0:slot0 + nb, r * 128:(r + 1) * 128]
                tk.emit("dve", lambda q, pb=pb, dstv=dstv, nb=nb:
                        q.tensor_copy(dstv, pb[:, 0:nb * 128]),
                        reads=[bpb], writes=[bET])

        for r in range(NT):
            do_scores(r)
            if r >= 1:
                do_ET(r - 1)
        do_ET(NT - 1)

        vov = VO.rearrange("p (s c) -> p s c", s=NT)
        vpv = VP.rearrange("p (s c) -> p s c", s=NT)
        for half in range(2):
            t0h, t1h = half * 512, (half + 1) * 512
            pieces = []
            for side in range(2):
                for p in range(NT):
                    ta = p * 128
                    tb = T if layer == 1 else min(T, (p + 5) * 128)
                    a, b2 = max(ta, t0h), min(tb, t1h)
                    if a < b2:
                        pieces.append((side, p, a, b2))
            pieces.sort(key=lambda t: (t[2] - t0h != 0 or t[3] - t1h != 0))
            assert pieces[0][2] == t0h and pieces[0][3] == t1h, pieces[:2]

            def g(q, pieces=pieces, t0h=t0h, half=half, kv=kv):
                last = None
                for idx, (side, p, a, b2) in enumerate(pieces):
                    vsl = (vov if side == 0 else vpv)[
                        :, p, kv * 128:(kv + 1) * 128]
                    slot = p + 8 * side
                    rhs = ET[:, slot * 1024 + a:slot * 1024 + b2]
                    last = q.matmul(
                        psE[half][0][:, a - t0h:b2 - t0h], vsl, rhs,
                        start=(idx == 0), stop=(idx == len(pieces) - 1),
                        skip_group_check=True)
                return last
            tk.emit("pe", g, reads=[bET, bVO, bVP], writes=[psE[half][1]])
            tk.emit("dve", lambda q, half=half, n=n: q.tensor_copy(
                ENC[:, n * 1024 + half * 512:n * 1024 + (half + 1) * 512],
                psE[half][0]), reads=[psE[half][1]], writes=[bENC])

    bX1 = Buf("x1hbm")
    WOS = [mem.sb(f"WOS{i}", [128, 512], BF16) for i in range(2)]
    OSTG = [mem.sb(f"OSTG{i}", [128, 512], F32) for i in range(1)]
    OSTG.append(OSTG[0])
    for gidx in range(2):
        tts = [gidx * 4 + i for i in range(4)]
        for ds in range(4):
            for fc in range(16):
                Wa, bWa = WOS[fc % 2]
                tk.emit("pl", lambda q, fc=fc, ds=ds, Wa=Wa: q.dma_start(
                    out=Wa, in_=WoT[fc * 128:(fc + 1) * 128,
                                    ds * 512:(ds + 1) * 512]),
                    writes=[bWa], dma=pfx + "wo%d" % (fc % 2))

                def g(q, fc=fc, Wa=Wa, tts=tts):
                    last = None
                    for i, tt in enumerate(tts):
                        last = q.matmul(
                            S4[:, i * 512:(i + 1) * 512],
                            ENC[:, fc * 128 + tt * 0:0] if False else
                            ENC[:, fc * 1024 + tt * 128:
                                fc * 1024 + (tt + 1) * 128],
                            Wa, start=(fc == 0), stop=(fc == 15),
                            skip_group_check=True)
                    return last
                tk.emit("pe", g, reads=[bENC, bWa],
                        writes=[bS4[i] for i in range(4)])
            for i, tt in enumerate(tts):
                Oa, bOa = OSTG[i % 2]
                tk.emit("dve", lambda q, i=i, Oa=Oa: q.tensor_copy(
                    Oa, S4[:, i * 512:(i + 1) * 512]),
                    reads=[bS4[i]], writes=[bOa])
                tk.emit("sp", lambda q, tt=tt, ds=ds, Oa=Oa: q.dma_start(
                    out=x1_o[tt * 128:(tt + 1) * 128, ds * 512:(ds + 1) * 512],
                    in_=Oa), reads=[bOa], writes=[bX1], dma=pfx + "os%d" % (i % 2))

    XLN1, bXLN1 = ENC[:, 0:D], bENC
    J1B = ENC[:, D:2 * D]
    XST, bXST = EE, bEE
    PART = ET.bitcast(F32)[:, 0:D]
    X1T = ET.bitcast(F32)[:, D:2 * D]
    X = [(ET.bitcast(F32)[:, 2 * D:3 * D], bET)]
    for r in range(NT):
        Xa, bXa = X[0]
        tk.emit("sp", lambda q, r=r: q.dma_start(
            out=PART, in_=x1_o[r * 128:(r + 1) * 128, :]),
            reads=[bX1], writes=[bET], dma=pfx + "x3p")
        tk.emit("sp", lambda q, r=r, Xa=Xa: q.dma_start(
            out=Xa, in_=xs[r * 128:(r + 1) * 128, :]),
            writes=[bXa], dma=pfx + "x3x")
        tk.emit("dve", lambda q, Xa=Xa: q.tensor_tensor(
            X1T, Xa, PART, op=AluOpType.add),
            reads=[bXa, bET], writes=[bET])
        tk.emit("sp", lambda q, r=r: q.dma_start(
            out=x1_o[r * 128:(r + 1) * 128, :], in_=X1T),
            reads=[bET], writes=[bX1], dma=pfx + "x1w")
        emit_ln(tk, st, X1T, bET, XLN1, bXLN1, (J1B, bENC), SSp[0])
        emit_tr16(tk, ident, b_id, XLN1, bXLN1, XST, bXST, 16, psT)
        tk.emit("sp", lambda q, r=r: q.dma_start(
            out=xlnT_o[r], in_=XST), reads=[bXST], dma=pfx + "xst")

    assert mem.bpp <= 207000, f"SBUF over budget: {mem.bpp}"


def build_B(layer):
    nc = bass.Bass("TRN2", target_bir_lowering=False)
    nk = 8 if layer == 0 else 16
    io = {}
    if layer == 0:
        io["qT"] = nc.dram_tensor("qT", [NT, 128, 2048], BF16, kind="ExternalInput").ap()
        io["kTo"] = nc.dram_tensor("kTo", [NT, 128, nk * 128], BF16, kind="ExternalInput").ap()
        io["kTp"] = nc.dram_tensor("kTp", [NT, 128, nk * 128], BF16, kind="ExternalInput").ap()
    else:
        io["qT"] = nc.dram_tensor("qT", [16, 128, T], BF16, kind="ExternalInput").ap()
        io["kTo"] = nc.dram_tensor("kTo", [nk, 128, T], BF16, kind="ExternalInput").ap()
        io["kTp"] = nc.dram_tensor("kTp", [nk, 128, T], BF16, kind="ExternalInput").ap()
    io["vo"] = nc.dram_tensor("vo", [NT, 128, nk * 128], BF16, kind="ExternalInput").ap()
    io["vp"] = nc.dram_tensor("vp", [NT, 128, nk * 128], BF16, kind="ExternalInput").ap()
    io["xs"] = nc.dram_tensor("xs", [T, D], F32, kind="ExternalInput").ap()
    io["WoT"] = nc.dram_tensor("WoT", [2048, D], F32, kind="ExternalInput").ap()
    io["msc"] = nc.dram_tensor("msc", [128, 2], F32, kind="ExternalInput").ap()
    io["x1"] = nc.dram_tensor("x1", [T, D], F32, kind="ExternalOutput").ap()
    io["xlnT"] = nc.dram_tensor("xlnT", [NT, 128, 2048], BF16, kind="ExternalOutput").ap()
    with ExitStack() as gctx:
        tk = TK(nc, gctx)
        with ExitStack() as ctx:
            emit_B(nc, tk, ctx, layer, io)
        tk.finish()
    return nc




def host_prep_B(layer, inputs, qT, kT, v, xs_by_core):
    nc = build_B(layer)
    if layer == 0:
        WoT = np.ascontiguousarray(
            inputs["L0_attn_out_kernel"].reshape(2048, D))
    else:
        WoT = np.ascontiguousarray(inputs["L1_attn_W_O"].T)
    in_maps = []
    for c in range(8):
        h = c % 2
        even, odd = c - h, c - h + 1
        own, par = (even, odd) if h == 0 else (odd, even)
        m0 = NEG if h == 0 else 0.0
        m1 = 0.0 if h == 0 else NEG
        msc = np.zeros((128, 2), np.float32)
        msc[:, 0] = m0
        msc[:, 1] = m1
        in_maps.append({
            "qT": qT[c], "kTo": kT[own], "kTp": kT[par],
            "vo": v[own], "vp": v[par], "xs": xs_by_core[c],
            "WoT": WoT, "msc": msc})
    return nc, in_maps





def emit_C0(nc, tk, ctx, half, io, pfx=""):
    xlnT_i = io["xlnT"]; Wg = io["Wg"]; Wu = io["Wu"]; Wd = io["Wd"]
    out_o = io["out"]
    if half == 1:
        part_i = io["part"]; x1_i = io["x1"]
    mem = Mem(nc, ctx)
    ident, b_id = emit_identity(tk, mem)

    XLT, bXLT = mem.sb("XLT", [128, 16384], BF16)
    tk.emit("sp", lambda q: q.dma_start(
        out=XLT.rearrange("p (r c) -> p r c", r=NT),
        in_=xlnT_i.rearrange("r p c -> p r c")), writes=[bXLT], dma=pfx + "xlt")

    H = [mem.sb(f"H{r}", [128, 4096], BF16) for r in range(NT)]
    HS, bHS = mem.sb("HS", [128, 4096], BF16)
    WGs = [mem.sb(f"WGs{i}", [128, 16 * 512], BF16) for i in range(2)]
    WUs = [mem.sb(f"WUs{i}", [128, 16 * 512], BF16) for i in range(2)]
    WDs = [mem.sb(f"WDs{i}", [128, 512], BF16) for i in range(2)]
    GSc = [mem.sb(f"GS{i}", [128, 512], F32) for i in range(2)]
    GS, bGS = GSc[0]
    SCR = [mem.sb(f"SCR{i}", [128, 1024], BF16) for i in range(6)]
    OST = [mem.sb(f"OST{i}", [128, 512], F32) for i in range(2)]
    AH, bAH = mem.sb("AH", [128, 4096], BF16)

    psD = [mem.ps(f"psD{i}", [128, 512], F32) for i in range(6)]
    psT = [mem.ps(f"psTt{i}", [128, 1024], BF16) for i in range(2)]

    for s in range(8):
        WGa, bWGa = WGs[s % 2]
        WUa, bWUa = WUs[s % 2]
        tk.emit("pl", lambda q, s=s, WGa=WGa: q.dma_start(
            out=WGa.rearrange("p (c i) -> p c i", c=16),
            in_=Wg[s]),
            writes=[bWGa], dma=pfx + "wg%d" % (s % 2))
        tk.emit("pl", lambda q, s=s, WUa=WUa: q.dma_start(
            out=WUa.rearrange("p (c i) -> p c i", c=16),
            in_=Wu[s]),
            writes=[bWUa], dma=pfx + "wu%d" % (s % 2))
        for r in range(NT):
            pg, bpg = psD[(2 * r) % 4]
            pu, bpu = psD[(2 * r) % 4 + 1]

            def gmm(q, W, pg, r):
                last = None
                for c in range(16):
                    last = q.matmul(
                        pg, XLT[:, r * 2048 + c * 128:r * 2048 + (c + 1) * 128],
                        W[:, c * 512:(c + 1) * 512],
                        start=(c == 0), stop=(c == 15))
                return last
            tk.emit("pe", lambda q, WGa=WGa, pg=pg, r=r: gmm(q, WGa, pg, r),
                    reads=[bXLT, bWGa], writes=[bpg])
            tk.emit("pe", lambda q, WUa=WUa, pu=pu, r=r: gmm(q, WUa, pu, r),
                    reads=[bXLT, bWUa], writes=[bpu])
            GSa, bGSa = GSc[r % 2]
            tk.emit("act", lambda q, pg=pg, GSa=GSa: q.activation(
                GSa, pg, AF.Silu), reads=[bpg], writes=[bGSa])
            Ha, bHa = H[r]
            hv = Ha[:, s * 512:(s + 1) * 512].rearrange("p (m g) -> p g m", m=4)
            tk.emit("dve", lambda q, pu=pu, hv=hv, GSa=GSa: q.tensor_tensor(
                hv, GSa, pu, op=AluOpType.mult),
                reads=[bGSa, bpu], writes=[bHa])

    for r in range(NT):
        Ha, bHa = H[r]
        h4 = Ha.rearrange("p (s m g) -> p s m g", s=8, m=4)
        a4 = AH.rearrange("p (s m g) -> p s m g", s=8, m=4)
        tk.emit("act", lambda q, Ha=Ha: q.activation(AH, Ha, AF.Abs),
                reads=[bHa], writes=[bAH])
        C = {}
        bC = {}
        pairs = [(0, 1), (0, 2), (0, 3), (1, 2), (1, 3), (2, 3)]
        for idx, (i, j) in enumerate(pairs):
            Cc, bCc = SCR[idx]
            tk.emit("dve", lambda q, Cc=Cc, i=i, j=j, a4=a4: q.tensor_tensor(
                Cc.rearrange("p (s g) -> p s g", s=8),
                a4[:, :, i:i + 1, :].squeeze(2),
                a4[:, :, j:j + 1, :].squeeze(2), op=AluOpType.is_ge),
                reads=[bAH], writes=[bCc])
            C[(i, j)] = Cc
            bC[(i, j)] = bCc
        U, V, W2t, X3t = (AH[:, 0:1024], AH[:, 1024:2048],
                          AH[:, 2048:3072], AH[:, 3072:4096])
        bU = bV = bW2t = bX3t = bAH
        tk.emit("dve", lambda q: q.tensor_tensor(
            U, C[(0, 1)], C[(0, 2)], op=AluOpType.add),
            reads=[bC[(0, 1)], bC[(0, 2)]], writes=[bU])
        tk.emit("dve", lambda q: q.tensor_tensor(
            U, U, C[(0, 3)], op=AluOpType.add),
            reads=[bU, bC[(0, 3)]], writes=[bU])
        tk.emit("dve", lambda q: q.tensor_tensor(
            V, C[(1, 2)], C[(1, 3)], op=AluOpType.add),
            reads=[bC[(1, 2)], bC[(1, 3)]], writes=[bV])
        tk.emit("dve", lambda q: q.tensor_tensor(
            V, V, C[(0, 1)], op=AluOpType.subtract),
            reads=[bV, bC[(0, 1)]], writes=[bV])
        tk.emit("dve", lambda q: q.tensor_tensor(
            W2t, C[(2, 3)], C[(0, 2)], op=AluOpType.subtract),
            reads=[bC[(2, 3)], bC[(0, 2)]], writes=[bW2t])
        tk.emit("dve", lambda q: q.tensor_tensor(
            W2t, W2t, C[(1, 2)], op=AluOpType.subtract),
            reads=[bW2t, bC[(1, 2)]], writes=[bW2t])
        tk.emit("dve", lambda q: q.tensor_tensor(
            X3t, C[(0, 3)], C[(1, 3)], op=AluOpType.add),
            reads=[bC[(0, 3)], bC[(1, 3)]], writes=[bX3t])
        tk.emit("dve", lambda q: q.tensor_tensor(
            X3t, X3t, C[(2, 3)], op=AluOpType.add),
            reads=[bX3t, bC[(2, 3)]], writes=[bX3t])
        hs4 = HS.rearrange("p (s m g) -> p s m g", s=8, m=4)
        thr = [(U, bU, 1.5, AluOpType.is_ge), (V, bV, 0.5, AluOpType.is_ge),
               (W2t, bW2t, -0.5, AluOpType.is_ge), (X3t, bX3t, 1.5, AluOpType.is_le)]
        for m in range(4):
            Bt, bBt, tv, opx = thr[m]
            tk.emit("dve", lambda q, Bt=Bt, tv=tv, opx=opx, m=m, h4=h4, hs4=hs4:
                    q.scalar_tensor_tensor(
                hs4[:, :, m:m + 1, :].squeeze(2),
                Bt.rearrange("p (s g) -> p s g", s=8), tv,
                h4[:, :, m:m + 1, :].squeeze(2),
                op0=opx, op1=AluOpType.mult),
                reads=[bBt, bHa], writes=[bHS])
        emit_tr16(tk, ident, b_id, HS, bHS, Ha, bHa, 32, psT)

    if half == 1:
        PIN = XLT.bitcast(F32)[:, 0:D]
        X1b = XLT.bitcast(F32)[:, D:2 * D]
    for grp in ([0, 1, 2, 3, 4, 5], [6, 7]):
        for ds in range(4):
            for cc in range(32):
                WDa, bWDa = WDs[cc % 2]
                tk.emit("pl", lambda q, cc=cc, ds=ds, WDa=WDa: q.dma_start(
                    out=WDa, in_=Wd[cc * 128:(cc + 1) * 128,
                                    ds * 512:(ds + 1) * 512]),
                    writes=[bWDa], dma=pfx + "wd%d" % (cc % 2))

                def g(q, cc=cc, WDa=WDa, grp=grp):
                    last = None
                    for gi, r in enumerate(grp):
                        last = q.matmul(
                            psD[gi][0],
                            H[r][0][:, cc * 128:(cc + 1) * 128], WDa,
                            start=(cc == 0), stop=(cc == 31),
                            skip_group_check=True)
                    return last
                tk.emit("pe", g, reads=[bWDa] + [H[r][1] for r in grp],
                        writes=[psD[gi][1] for gi in range(len(grp))])
            for gi, r in enumerate(grp):
                Oa, bOa = OST[gi % 2]
                if half == 0:
                    tk.emit("dve", lambda q, gi=gi, Oa=Oa: q.tensor_copy(
                        Oa, psD[gi][0]), reads=[psD[gi][1]], writes=[bOa])
                else:
                    tk.emit("sp", lambda q, r=r, ds=ds: q.dma_start(
                        out=PIN[:, 0:512], in_=part_i[
                            r * 128:(r + 1) * 128, ds * 512:(ds + 1) * 512]),
                        writes=[bXLT], dma=pfx + "pin")
                    tk.emit("sp", lambda q, r=r, ds=ds: q.dma_start(
                        out=X1b[:, 0:512], in_=x1_i[
                            r * 128:(r + 1) * 128, ds * 512:(ds + 1) * 512]),
                        writes=[bXLT], dma=pfx + "pin")
                    tk.emit("dve", lambda q, gi=gi: q.tensor_tensor(
                        GS, psD[gi][0], PIN[:, 0:512], op=AluOpType.add),
                        reads=[psD[gi][1], bXLT], writes=[bGS])
                    tk.emit("dve", lambda q, Oa=Oa: q.tensor_tensor(
                        Oa, GS, X1b[:, 0:512], op=AluOpType.add),
                        reads=[bGS, bXLT], writes=[bOa])
                tk.emit("sp", lambda q, r=r, ds=ds, Oa=Oa: q.dma_start(
                    out=out_o[r * 128:(r + 1) * 128, ds * 512:(ds + 1) * 512],
                    in_=Oa), reads=[bOa], dma=pfx + "ost%d" % (gi % 2))

    assert mem.bpp <= 207000, f"SBUF over budget: {mem.bpp}"


def build_C0(half):
    nc = bass.Bass("TRN2", target_bir_lowering=False)
    io = {}
    io["xlnT"] = nc.dram_tensor("xlnT", [NT, 128, 2048], BF16, kind="ExternalInput").ap()
    io["Wg"] = nc.dram_tensor("Wg", [8, 128, 16, 512], F32, kind="ExternalInput").ap()
    io["Wu"] = nc.dram_tensor("Wu", [8, 128, 16, 512], F32, kind="ExternalInput").ap()
    io["Wd"] = nc.dram_tensor("Wd", [4096, D], F32, kind="ExternalInput").ap()
    if half == 1:
        io["part"] = nc.dram_tensor("part", [T, D], F32, kind="ExternalInput").ap()
        io["x1"] = nc.dram_tensor("x1", [T, D], F32, kind="ExternalInput").ap()
    io["out"] = nc.dram_tensor("out", [T, D], F32, kind="ExternalOutput").ap()
    with ExitStack() as gctx:
        tk = TK(nc, gctx)
        with ExitStack() as ctx:
            emit_C0(nc, tk, ctx, half, io)
        tk.finish()
    return nc




def emit_C1(nc, tk, ctx, io, pfx=""):
    xlnT_i = io["xlnT"]; W13 = io["W13"]; W2T = io["W2T"]
    H128i = io["H128"]; x1_i = io["x1"]; out_o = io["out"]
    NC = 64
    mem = Mem(nc, ctx)

    XLT, bXLT = mem.sb("XLT", [128, 16384], BF16)
    tk.emit("sp", lambda q: q.dma_start(
        out=XLT.rearrange("p (r c) -> p r c", r=NT),
        in_=xlnT_i.rearrange("r p c -> p r c")), writes=[bXLT], dma=pfx + "xlt")
    H128S, bH128 = mem.sb("H128S", [128, 128], BF16)
    tk.emit("pl", lambda q: q.dma_start(out=H128S, in_=H128i),
            writes=[bH128], dma=pfx + "h128")

    HF, _ = mem.sb("HF", [128, 66 * 1024], BF16)
    bslot = [Buf(f"hf{i}") for i in range(66)]
    slot_of = list(range(NC))
    free_slots = [64, 65]

    W13s = [mem.sb(f"W13s{i}", [128, 16 * 256], BF16) for i in range(2)]
    W2s = [mem.sb(f"W2s{i}", [128, 512], BF16) for i in range(2)]
    GSp = [mem.sb(f"GS{i}", [128, 512], F32) for i in range(2)]
    OST = [mem.sb(f"OST{i}", [128, 512], F32) for i in range(2)]
    PIN = [mem.sb(f"PIN{i}", [128, 512], F32) for i in range(2)]
    psD = [mem.ps(f"psD{i}", [128, 512], F32) for i in range(8)]
    xv = XLT.rearrange("p (r c t) -> p c r t", c=16, t=128)

    def bfly(a, b2):
        sa, sb = slot_of[a], slot_of[b2]
        s1, s2 = free_slots[0], free_slots[1]
        A = HF[:, sa * 1024:(sa + 1) * 1024]
        B = HF[:, sb * 1024:(sb + 1) * 1024]
        U = HF[:, s1 * 1024:(s1 + 1) * 1024]
        V = HF[:, s2 * 1024:(s2 + 1) * 1024]
        tk.emit("dve", lambda q, A=A, B=B, U=U: q.tensor_tensor(
            U, A, B, op=AluOpType.add),
            reads=[bslot[sa], bslot[sb]], writes=[bslot[s1]])
        tk.emit("dve", lambda q, A=A, B=B, V=V: q.tensor_tensor(
            V, A, B, op=AluOpType.subtract),
            reads=[bslot[sa], bslot[sb]], writes=[bslot[s2]])
        slot_of[a], slot_of[b2] = s1, s2
        free_slots[0], free_slots[1] = sa, sb

    pend = []

    def do_h128(c):
        slot = slot_of[c]
        hc = HF[:, slot * 1024:(slot + 1) * 1024]
        for m in range(2):
            ph, bph = psD[4 + m]
            tk.emit("pe", lambda q, ph=ph, hc=hc, m=m: q.matmul(
                ph, H128S, hc[:, m * 512:(m + 1) * 512],
                start=True, stop=True),
                reads=[bH128, bslot[slot]], writes=[bph])
            tk.emit("act", lambda q, ph=ph, hc=hc, m=m: q.activation(
                hc[:, m * 512:(m + 1) * 512], ph, AF.Copy,
                scale=2.0 ** -6.5), reads=[bph], writes=[bslot[slot]])
        for s in range(6):
            blk = 1 << (s + 1)
            if (c + 1) % blk == 0:
                step = 1 << s
                for a in range(c + 1 - blk, c + 1):
                    if not (a & step):
                        bfly(a, a | step)

    for c in range(NC):
        Wa, bWa = W13s[c % 2]
        tk.emit("pl", lambda q, c=c, Wa=Wa: q.dma_start(
            out=Wa.rearrange("p (k x) -> p k x", k=16),
            in_=W13[c]),
            writes=[bWa], dma=pfx + "w13_%d" % (c % 2))
        slot = slot_of[c]
        hc = HF[:, slot * 1024:(slot + 1) * 1024]
        for m in range(2):
            pg, bpg = psD[2 * (c % 2)]
            pu, bpu = psD[2 * (c % 2) + 1]
            GS, bGS = GSp[m]

            def gmm(q, which, pb, m, Wa):
                last = None
                for k in range(16):
                    last = q.matmul(
                        pb, Wa[:, k * 256 + which * 128:
                               k * 256 + (which + 1) * 128],
                        xv[:, k, m * 4:(m + 1) * 4, :],
                        start=(k == 0), stop=(k == 15))
                return last
            tk.emit("pe", lambda q, pg=pg, m=m, Wa=Wa: gmm(q, 0, pg, m, Wa),
                    reads=[bXLT, bWa], writes=[bpg])
            tk.emit("pe", lambda q, pu=pu, m=m, Wa=Wa: gmm(q, 1, pu, m, Wa),
                    reads=[bXLT, bWa], writes=[bpu])
            tk.emit("act", lambda q, pg=pg, GS=GS: q.activation(GS, pg, AF.Silu),
                    reads=[bpg], writes=[bGS])
            tk.emit("dve", lambda q, pu=pu, hc=hc, m=m, GS=GS: q.tensor_tensor(
                hc[:, m * 512:(m + 1) * 512], GS, pu, op=AluOpType.mult),
                reads=[bGS, bpu], writes=[bslot[slot]])
        if c >= 1:
            do_h128(c - 1)
    do_h128(NC - 1)

    for ds in range(4):
        for cc in range(NC):
            Wa, bWa = W2s[cc % 2]
            tk.emit("pl", lambda q, cc=cc, ds=ds, Wa=Wa: q.dma_start(
                out=Wa, in_=W2T[cc * 128:(cc + 1) * 128,
                                ds * 512:(ds + 1) * 512]),
                writes=[bWa], dma=pfx + "w2_%d" % (cc % 2))
            slot = slot_of[cc]

            def g(q, slot=slot, Wa=Wa, cc=cc):
                last = None
                for r in range(NT):
                    last = q.matmul(
                        psD[r][0],
                        HF[:, slot * 1024 + r * 128:slot * 1024 + (r + 1) * 128],
                        Wa, start=(cc == 0), stop=(cc == NC - 1),
                        skip_group_check=True)
                return last
            tk.emit("pe", g, reads=[bWa, bslot[slot]],
                    writes=[psD[r][1] for r in range(NT)])
        for r in range(NT):
            Pa, bPa = PIN[r % 2]
            Oa, bOa = OST[r % 2]
            tk.emit("sp", lambda q, r=r, ds=ds, Pa=Pa: q.dma_start(
                out=Pa, in_=x1_i[r * 128:(r + 1) * 128,
                                 ds * 512:(ds + 1) * 512]),
                writes=[bPa], dma=pfx + "pin%d" % (r % 2))
            tk.emit("dve", lambda q, r=r, Oa=Oa, Pa=Pa: q.tensor_tensor(
                Oa, psD[r][0], Pa, op=AluOpType.add),
                reads=[psD[r][1], bPa], writes=[bOa])
            tk.emit("sp", lambda q, r=r, ds=ds, Oa=Oa: q.dma_start(
                out=out_o[r * 128:(r + 1) * 128, ds * 512:(ds + 1) * 512],
                in_=Oa), reads=[bOa], dma=pfx + "ost%d" % (r % 2))

    assert mem.bpp <= 207000, f"SBUF over budget: {mem.bpp}"


def build_C1():
    nc = bass.Bass("TRN2", target_bir_lowering=False)
    io = {}
    io["xlnT"] = nc.dram_tensor("xlnT", [NT, 128, 2048], BF16, kind="ExternalInput").ap()
    io["W13"] = nc.dram_tensor("W13", [64, 128, 16, 256], F32, kind="ExternalInput").ap()
    io["W2T"] = nc.dram_tensor("W2T", [8192, D], F32, kind="ExternalInput").ap()
    io["H128"] = nc.dram_tensor("H128", [128, 128], F32, kind="ExternalInput").ap()
    io["x1"] = nc.dram_tensor("x1", [T, D], F32, kind="ExternalInput").ap()
    io["out"] = nc.dram_tensor("out", [T, D], F32, kind="ExternalOutput").ap()
    with ExitStack() as gctx:
        tk = TK(nc, gctx)
        with ExitStack() as ctx:
            emit_C1(nc, tk, ctx, io)
        tk.finish()
    return nc




def _hadamard128():
    i = np.arange(128)
    pc = np.zeros((128, 128), np.int64)
    x = i[:, None] & i[None, :]
    while x.any():
        pc += x & 1
        x >>= 1
    return np.where(pc % 2 == 0, 1.0, -1.0).astype(np.float32)


def plane_perm_rows(wd_half):
    w = wd_half.reshape(8, 128, 4, wd_half.shape[1])
    return np.ascontiguousarray(w.transpose(0, 2, 1, 3).reshape(4096, -1))


def host_prep_C0(half, inputs, xlnT, x1=None, part=None):
    nc = build_C0(half)
    Wg = _pmaj_slab(inputs["L0_ffn_w_gate"][:, half * 4096:(half + 1) * 4096])
    Wu = _pmaj_slab(inputs["L0_ffn_w_up"][:, half * 4096:(half + 1) * 4096])
    Wd = plane_perm_rows(inputs["L0_ffn_w_down"][half * 4096:(half + 1) * 4096])
    in_maps = []
    for c in range(8):
        m = {"xlnT": xlnT[c], "Wg": Wg, "Wu": Wu, "Wd": Wd}
        if half == 1:
            m["part"] = part[c]
            m["x1"] = x1[c]
        in_maps.append(m)
    return nc, in_maps


def host_prep_C1(inputs, xlnT, x1):
    nc = build_C1()
    W1T = inputs["L1_ffn_W1"].T
    W3T = inputs["L1_ffn_W3"].T
    W13 = np.concatenate([
        W1T.reshape(D, 64, 128).transpose(1, 0, 2),
        W3T.reshape(D, 64, 128).transpose(1, 0, 2)], axis=2)
    W13 = _pmaj16h(W13)
    W2T = np.ascontiguousarray(inputs["L1_ffn_W2"].T)
    H128 = _hadamard128()
    in_maps = [{"xlnT": xlnT[c], "W13": W13, "W2T": W2T, "H128": H128,
                "x1": x1[c]} for c in range(8)]
    return nc, in_maps


def kernel(**inputs):
    return kernel_fused(**inputs)


def kernel_unfused(**inputs):
    x = np.ascontiguousarray(inputs["L0_attn_x"], dtype=np.float32)
    xs = shard_x(x)
    cores = list(range(8))

    nc, maps = host_prep_A(0, inputs, xs)
    res = run_bass_kernel_spmd(nc, maps, core_ids=cores)
    qT = [res.results[c]["qT"] for c in cores]
    kT = [res.results[c]["kT"] for c in cores]
    v = [res.results[c]["v"] for c in cores]
    nc, maps = host_prep_B(0, inputs, qT, kT, v, xs)
    res = run_bass_kernel_spmd(nc, maps, core_ids=cores)
    x1 = [res.results[c]["x1"] for c in cores]
    xlnT = [res.results[c]["xlnT"] for c in cores]
    nc, maps = host_prep_C0(0, inputs, xlnT)
    res = run_bass_kernel_spmd(nc, maps, core_ids=cores)
    part = [res.results[c]["out"] for c in cores]
    nc, maps = host_prep_C0(1, inputs, xlnT, x1=x1, part=part)
    res = run_bass_kernel_spmd(nc, maps, core_ids=cores)
    xs = [res.results[c]["out"] for c in cores]

    nc, maps = host_prep_A(1, inputs, xs)
    res = run_bass_kernel_spmd(nc, maps, core_ids=cores)
    qT = [res.results[c]["qT"] for c in cores]
    kT = [res.results[c]["kT"] for c in cores]
    v = [res.results[c]["v"] for c in cores]
    nc, maps = host_prep_B(1, inputs, qT, kT, v, xs)
    res = run_bass_kernel_spmd(nc, maps, core_ids=cores)
    x1 = [res.results[c]["x1"] for c in cores]
    xlnT = [res.results[c]["xlnT"] for c in cores]
    nc, maps = host_prep_C1(inputs, xlnT, x1)
    res = run_bass_kernel_spmd(nc, maps, core_ids=cores)
    out = [res.results[c]["out"] for c in cores]
    return unshard_x(out)



RGROUPS = [[0, 1], [2, 3], [4, 5], [6, 7]]


def barrier(tk):
    for e in ENGS:
        tk.wait_all_on(e)


def emit_contrib(nc, tk, ctx, layer, io, pfx=""):
    nk = 8 if layer == 0 else 16
    mem = Mem(nc, ctx)
    kT = io["kT"]
    v = io["v"]
    rsin = io["rsin"]
    rsout = io["rsout"]
    cm = io["cmask"]
    HALF = rsin.shape[0] // 2
    KU = kT.shape[0]
    CM, bCM = mem.sb(pfx + "CM", [128, 2], F32)
    tk.emit("sp", lambda q: q.dma_start(out=CM, in_=cm), writes=[bCM],
            dma=pfx + "cm")
    TL, bTL = mem.sb(pfx + "TL", [128, 2048], BF16)
    TM, bTM = mem.sb(pfx + "TM", [128, 2048], BF16)
    b_rsin = io.setdefault("_b_rsin", Buf("rsin"))

    def unit(src_ap, rows0, width):
        tk.emit("sp", lambda q: q.dma_start(out=TL[:, 0:width], in_=src_ap),
                writes=[bTL], dma=pfx + "tl")
        for s in range(2):
            tk.emit("dve", lambda q, s=s: q.tensor_scalar(
                TM[:, 0:width], TL[:, 0:width], CM[:, s:s + 1], None,
                op0=AluOpType.mult), reads=[bTL, bCM], writes=[bTM])
            nrow = (128 * width) // 1024
            dst = rsin[s * HALF + rows0:s * HALF + rows0 + nrow, :]
            dstv = dst.rearrange("(p two) c -> p (two c)", p=128) \
                if width == 2048 else dst.rearrange("(p one) c -> p (one c)", p=128)
            tk.emit("sp", lambda q, dstv=dstv, width=width: q.dma_start(
                out=dstv, in_=TM[:, 0:width]),
                reads=[bTM], writes=[b_rsin], dma=pfx + "rs")

    for u in range(KU):
        unit(kT[u], u * 128, kT.shape[2])
    krows = KU * (kT.shape[2] * 128 // 1024)
    vw = nk * 128
    for u in range(NT):
        unit(v[u], krows + u * (128 * vw // 1024), vw)

    b_rsout = io.setdefault("_b_rsout", Buf("rsout"))
    tk.emit("pl", lambda q: q.collective_compute(
        "ReduceScatter", AluOpType.add, replica_groups=RGROUPS,
        ins=[rsin.opt()], outs=[rsout.opt()]),
        reads=[b_rsin], writes=[b_rsout])
    io["b_rsout"] = b_rsout


def build_fused():
    nc = bass.Bass("TRN2", target_bir_lowering=False)
    E = lambda n, s, d: nc.dram_tensor(n, s, d, kind="ExternalInput").ap()
    I = lambda n, s, d: nc.dram_tensor(n, s, d).ap()
    xs = E("xs", [T, D], F32)
    out_o = nc.dram_tensor("out", [T, D], F32, kind="ExternalOutput").ap()
    ins = dict(
        l0Wq=E("l0Wq", [128, 16, 2048], F32), l0Wk=E("l0Wk", [128, 16, 1024], F32),
        l0Wv=E("l0Wv", [128, 16, 1024], F32), cosT=E("cosT", [T, 64], F32),
        sinT=E("sinT", [T, 64], F32), qns=E("qns", [1, 128], F32),
        kns=E("kns", [1, 128], F32), l0Wo=E("l0Wo", [2048, D], F32),
        msc=E("msc", [128, 2], F32), cmask=E("cmask", [128, 2], F32),
        l0Wg=E("l0Wg", [16, 128, 16, 512], F32), l0Wu=E("l0Wu", [16, 128, 16, 512], F32),
        l0Wd=E("l0Wd", [8192, D], F32),
        l1Wq=E("l1Wq", [16, 128, 16, 128], F32), l1Wk=E("l1Wk", [16, 128, 16, 128], F32),
        l1Wv=E("l1Wv", [16, 128, 16, 128], F32), l1Wo=E("l1Wo", [2048, D], F32),
        W13=E("W13", [64, 128, 16, 256], F32), W2T=E("W2T", [8192, D], F32),
        H128=E("H128", [128, 128], F32),
    )
    qT0 = I("qT0", [NT, 128, 2048], BF16)
    kT0 = I("kT0", [NT, 128, 1024], BF16)
    v0 = I("v0", [NT, 128, 1024], BF16)
    rsin0 = I("rsin0", [2 * 2048, 1024], BF16)
    rsout0 = I("rsout0", [2048, 1024], BF16)
    x1_0 = I("x1_0", [T, D], F32)
    xlnT_0 = I("xlnT_0", [NT, 128, 2048], BF16)
    part0 = I("part0", [T, D], F32)
    x2_0 = I("x2_0", [T, D], F32)
    qT1 = I("qT1", [16, 128, T], BF16)
    kT1 = I("kT1", [16, 128, T], BF16)
    v1 = I("v1", [NT, 128, 2048], BF16)
    rsin1 = I("rsin1", [2 * 4096, 1024], BF16)
    rsout1 = I("rsout1", [4096, 1024], BF16)
    x1_1 = I("x1_1", [T, D], F32)
    xlnT_1 = I("xlnT_1", [NT, 128, 2048], BF16)

    with ExitStack() as gctx:
        tk = TK(nc, gctx)
        with ExitStack() as c:
            emit_A(nc, tk, c, 0, dict(
                xs=xs, Wq=ins["l0Wq"], Wk=ins["l0Wk"], Wv=ins["l0Wv"],
                cosT=ins["cosT"], sinT=ins["sinT"], qns=ins["qns"],
                kns=ins["kns"], qT=qT0, kT=kT0, v=v0), pfx="a0")
        barrier(tk)
        ioc0 = dict(kT=kT0, v=v0, rsin=rsin0, rsout=rsout0, cmask=ins["cmask"])
        with ExitStack() as c:
            emit_contrib(nc, tk, c, 0, ioc0, pfx="x0")
        barrier(tk)
        kTp0 = rsout0[0:1024].rearrange("(r p) c -> r p c", p=128)
        vp0 = rsout0[1024:2048].rearrange("(r p) c -> r p c", p=128)
        with ExitStack() as c:
            emit_B(nc, tk, c, 0, dict(
                qT=qT0, kTo=kT0, kTp=kTp0, vo=v0, vp=vp0, xs=xs,
                WoT=ins["l0Wo"], msc=ins["msc"], x1=x1_0, xlnT=xlnT_0),
                pfx="b0")
        barrier(tk)
        with ExitStack() as c:
            emit_C0(nc, tk, c, 0, dict(
                xlnT=xlnT_0, Wg=ins["l0Wg"][0:8], Wu=ins["l0Wu"][0:8],
                Wd=ins["l0Wd"][0:4096], out=part0), pfx="c0")
        barrier(tk)
        with ExitStack() as c:
            emit_C0(nc, tk, c, 1, dict(
                xlnT=xlnT_0, Wg=ins["l0Wg"][8:16],
                Wu=ins["l0Wu"][8:16], Wd=ins["l0Wd"][4096:8192],
                part=part0, x1=x1_0, out=x2_0), pfx="c1")
        barrier(tk)
        with ExitStack() as c:
            emit_A(nc, tk, c, 1, dict(
                xs=x2_0, Wq=ins["l1Wq"], Wk=ins["l1Wk"], Wv=ins["l1Wv"],
                qT=qT1, kT=kT1, v=v1), pfx="a1")
        barrier(tk)
        ioc1 = dict(kT=kT1, v=v1, rsin=rsin1, rsout=rsout1, cmask=ins["cmask"])
        with ExitStack() as c:
            emit_contrib(nc, tk, c, 1, ioc1, pfx="x1")
        barrier(tk)
        kTp1 = rsout1[0:2048].rearrange("(r p) c -> r p c", p=128)
        vp1 = rsout1[2048:4096].rearrange("(r p two) c -> r p (two c)", p=128, two=2)
        with ExitStack() as c:
            emit_B(nc, tk, c, 1, dict(
                qT=qT1, kTo=kT1, kTp=kTp1, vo=v1, vp=vp1, xs=x2_0,
                WoT=ins["l1Wo"], msc=ins["msc"], x1=x1_1, xlnT=xlnT_1),
                pfx="b1")
        barrier(tk)
        with ExitStack() as c:
            emit_C1(nc, tk, c, dict(
                xlnT=xlnT_1, W13=ins["W13"], W2T=ins["W2T"], H128=ins["H128"],
                x1=x1_1, out=out_o), pfx="d1")
        tk.finish()
    return nc


def host_prep_fused(inputs, xs_by_core):
    nc = build_fused()
    l0q = _pmaj16(inputs["L0_attn_q_kernel"].transpose(1, 0, 2).reshape(D, 2048))
    l0k = _pmaj16(inputs["L0_attn_kv_kernel"][0].transpose(1, 0, 2).reshape(D, 1024))
    l0v = _pmaj16(inputs["L0_attn_kv_kernel"][1].transpose(1, 0, 2).reshape(D, 1024))
    l0Wo = np.ascontiguousarray(inputs["L0_attn_out_kernel"].reshape(2048, D))
    l0Wd = np.concatenate([
        plane_perm_rows(inputs["L0_ffn_w_down"][0:4096]),
        plane_perm_rows(inputs["L0_ffn_w_down"][4096:8192])], axis=0)
    l1q = _pmaj16h(inputs["L1_attn_W_Q"].transpose(0, 2, 1))
    l1k = _pmaj16h(inputs["L1_attn_W_K"].transpose(0, 2, 1))
    l1v = _pmaj16h(inputs["L1_attn_W_V"].transpose(0, 2, 1))
    l1Wo = np.ascontiguousarray(inputs["L1_attn_W_O"].T)
    W1T = inputs["L1_ffn_W1"].T
    W3T = inputs["L1_ffn_W3"].T
    W13 = _pmaj16h(np.concatenate([
        W1T.reshape(D, 64, 128).transpose(1, 0, 2),
        W3T.reshape(D, 64, 128).transpose(1, 0, 2)], axis=2))
    W2T = np.ascontiguousarray(inputs["L1_ffn_W2"].T)
    H128 = _hadamard128()
    l0Wg_d = _pmaj_slab(inputs["L0_ffn_w_gate"])
    l0Wu_d = _pmaj_slab(inputs["L0_ffn_w_up"])
    qns = np.ascontiguousarray(inputs["L0_attn_q_norm_scale"].reshape(1, 128))
    kns = np.ascontiguousarray(inputs["L0_attn_k_norm_scale"].reshape(1, 128))
    in_maps = []
    for c in range(8):
        h = c % 2
        pos = (np.arange(T) // 128 * 2 + h) * 128 + (np.arange(T) % 128)
        j = np.arange(64)
        ts = 10000.0 ** (2 * j / 128.0)
        ang = pos[:, None] / ts[None, :]
        msc = np.zeros((128, 2), np.float32)
        msc[:, 0] = NEG if h == 0 else 0.0
        msc[:, 1] = 0.0 if h == 0 else NEG
        cmask = np.zeros((128, 2), np.float32)
        cmask[:, 0] = 1.0 if h == 1 else 0.0
        cmask[:, 1] = 1.0 if h == 0 else 0.0
        in_maps.append({
            "xs": xs_by_core[c], "l0Wq": l0q, "l0Wk": l0k, "l0Wv": l0v,
            "cosT": np.cos(ang).astype(np.float32),
            "sinT": np.sin(ang).astype(np.float32),
            "qns": qns, "kns": kns, "l0Wo": l0Wo, "msc": msc, "cmask": cmask,
            "l0Wg": l0Wg_d, "l0Wu": l0Wu_d,
            "l0Wd": l0Wd, "l1Wq": l1q, "l1Wk": l1k, "l1Wv": l1v,
            "l1Wo": l1Wo, "W13": W13, "W2T": W2T, "H128": H128})
    return nc, in_maps


def kernel_fused(**inputs):
    x = np.ascontiguousarray(inputs["L0_attn_x"], dtype=np.float32)
    xs = shard_x(x)
    nc, in_maps = host_prep_fused(inputs, xs)
    res = run_bass_kernel_spmd(nc, in_maps, core_ids=list(range(8)))
    return unshard_x([res.results[c]["out"] for c in range(8)])
